# version 12
# baseline (speedup 1.0000x reference)
import contextlib
import numpy as np
import ml_dtypes
import concourse.bass as bass
import concourse.mybir as mybir
from concourse.bass_utils import run_bass_kernel_spmd

F32 = mybir.dt.float32
BF16 = mybir.dt.bfloat16
I32 = mybir.dt.int32
I16 = mybir.dt.int16
I8 = mybir.dt.int8
AF = mybir.ActivationFunctionType
ALU = mybir.AluOpType
AX = mybir.AxisListType
BF = ml_dtypes.bfloat16

D = 4096
T = 4096
TH = 2048
TB = 1024
NB = T // TB
HD = 128
NQL = 16
NKVL = 4
QD = NQL * HD
KVD = NKVL * HD
OS = 512
NG = OS // 8

RS = 0.0883883476483184
ESHIFT = -12.0

ARENA_BYTES = 192 * 1024


class Sem:
    def __init__(self, cm, nc, name):
        self.h = cm.enter_context(nc.semaphore(name))
        self.n = 0

    def take(self, k=1):
        self.n += k
        return self.n


class Arena:

    def __init__(self, base_ap):
        self.base = base_ap
        self.off = 0

    def alloc(self, cols, dt):
        nbytes = cols * mybir.dt.size(dt)
        nbytes = (nbytes + 63) & ~63
        assert self.off + nbytes <= ARENA_BYTES, (self.off, nbytes)
        w0 = self.off // 4
        self.off += nbytes
        return self.base[:, w0:w0 + nbytes // 4].bitcast(dt)[:, 0:cols]

    def at(self, byte_off, cols, dt):
        nbytes = cols * mybir.dt.size(dt)
        assert byte_off % 4 == 0 and byte_off + nbytes <= ARENA_BYTES
        w0 = byte_off // 4
        return self.base[:, w0:w0 + (nbytes + 3) // 4].bitcast(dt)[:, 0:cols]

    def mark(self):
        return self.off

    def reset(self, off=0):
        self.off = off


def build(debug=(), detect_races=True):
    nc = bass.Bass("TRN2", target_bir_lowering=False,
                   detect_race_conditions=detect_races)

    def dram(name, shape, dt, kind=None):
        if kind:
            return nc.dram_tensor(name, shape, dt, kind=kind).ap()
        return nc.dram_tensor(name, shape, dt)

    x_in = dram("x", [T, D], F32, "ExternalInput")
    xh_in = dram("xh", [TH, D], F32, "ExternalInput")
    wq_in = dram("wq_s", [D, QD], F32, "ExternalInput")
    wk_in = dram("wk_s", [D, KVD], F32, "ExternalInput")
    wv_in = dram("wv_s", [D, KVD], F32, "ExternalInput")
    wo_in = dram("wo", [D, D], F32, "ExternalInput")
    qw_in = dram("qw_s", [8, OS, 128], I32, "ExternalInput")
    lut4_in = dram("lut4g_s", [NG, 128], F32, "ExternalInput")
    lut8_in = dram("lut8t_s", [OS, 256], F32, "ExternalInput")
    bias_in = dram("bias", [D], F32, "ExternalInput")
    ropeC_in = dram("ropeC", [HD, T], F32, "ExternalInput")
    ropeS_in = dram("ropeS", [HD, T], F32, "ExternalInput")
    perm_in = dram("permswap", [HD, HD], BF16, "ExternalInput")
    ident_in = dram("ident128", [HD, HD], BF16, "ExternalInput")
    ident32_in = dram("ident128f", [HD, HD], F32, "ExternalInput")
    dmask_in = dram("diagmasks", [HD, 8 * 1024], BF16, "ExternalInput")
    repl_in = dram("replones", [8, HD], BF16, "ExternalInput")
    sumones_in = dram("sumones", [HD, 8], BF16, "ExternalInput")
    mod16_in = dram("mod16", [HD, 1], F32, "ExternalInput")
    onesc_in = dram("onescol", [HD, 1], BF16, "ExternalInput")
    onesr_in = dram("onesrow", [1, HD], F32, "ExternalInput")
    onesrb_in = dram("onesrow_bf", [1, HD], BF16, "ExternalInput")
    hoff_in = dram("hoff", [1, 1], I32, "ExternalInput")

    out_ext = dram("out", [TH, D], F32, "ExternalOutput")

    xln_d = dram("xln_d", [T, D], BF16)
    q_d = dram("q_d", [QD, T], BF16)
    k_d = dram("k_d", [KVD, T], BF16)
    v_d = dram("v_d", [T, KVD], BF16)
    att_myg = [dram(f"att_myg{g}", [512, T], BF16) for g in range(4)]
    att_gg = [dram(f"att_gg{g}", [1024, T], BF16) for g in range(4)]
    wbf_d = dram("wbf_d", [24 * 128, D], BF16)
    xres_d = dram("xres_d", [TH, D], F32)
    xln2_d = dram("xln2_d", [TH, D], BF16)
    mask_d = dram("mask_d", [TH, 1], BF16)
    lorep_d = dram("lorep_d", [16 * OS, D], BF16)
    hirep_d = dram("hirep_d", [16 * OS, D], BF16)
    l8bd_d = dram("l8bd_d", [NG, 128, 128], BF16)
    wsh_d = dram("wsh_d", [OS, 2 * D], BF16)
    wshT_d = dram("wshT_d", [D, 1024], BF16)
    wfull_d = dram("wfull_d", [8 * D, 1024], BF16)

    dbg = {}
    dbg_l4bc = None
    if "l4bc" in debug:
        dbg_l4bc = dram("dbg_l4bc", [128, 64], F32, "ExternalOutput")
        debug = tuple(d for d in debug if d != "l4bc")
    dbg_deq = None
    if "deqdump" in debug:
        dbg_deq = dram("dbg_deq", [128, 1536], BF16, "ExternalOutput")
        debug = tuple(d for d in debug if d != "deqdump")
    for name in debug:
        src = {"xln": xln_d, "q": q_d, "k": k_d, "v": v_d, "att": att_myg[0],
               "attg": att_gg[0], "xres": xres_d, "xln2": xln2_d,
               "wfull": wfull_d, "wsh": wsh_d, "mask": mask_d}[name]
        a = src[:, :]
        dbg[name] = (dram("dbg_" + name, [a.shape[0], a.shape[1]], a.dtype,
                          "ExternalOutput"), src)

    ops = []

    def em(engine, fn):
        ops.append((engine, fn))

    def wait(engine, sem, val=None):
        v = sem.n if val is None else val
        if v > 0:
            em(engine, lambda e, s=sem.h, v=v: e.wait_ge(s, v))

    with contextlib.ExitStack() as stack:
        ent = stack.enter_context

        def sbuf(name, shape, dt):
            return ent(nc.sbuf_tensor(name, shape, dt))

        def psum(name, shape, dt=F32):
            return ent(nc.psum_tensor(name, shape, dt))

        S = lambda name: Sem(stack, nc, name)

        consts = sbuf("consts", [128, 16], F32)
        stats = sbuf("stats", [128, 16], F32)
        perm_t = sbuf("perm_t", [128, 128], BF16)
        ident_t = sbuf("ident_t", [128, 128], BF16)
        ident32_t = sbuf("ident32_t", [128, 128], F32)
        mod16_t = sbuf("mod16_t", [128, 1], F32)
        hoffs_t = sbuf("hoffs_t", [1, 1], I32)
        onesrb_t = sbuf("onesrb_t", [1, 128], BF16)
        repl_t = sbuf("repl_t", [8, 128], BF16)
        sumones_t = sbuf("sumones_t", [128, 8], BF16)
        lut4g_t = sbuf("lut4g_t", [64, 128], F32)
        lut4bc = sbuf("lut4bc", [128, 64], F32)
        arena_t = sbuf("arena", [128, ARENA_BYTES // 4], F32)
        AR = Arena(arena_t[:, :])

        p_a = psum("p_a", [128, TB])
        p_b = psum("p_b", [128, TB])
        p_c = psum("p_c", [128, TB])
        p_d = psum("p_d", [128, TB])

        sem_xb = [S("sx0"), S("sx1")]
        sem_lnb = [S("sln0"), S("sln1")]
        sem_xt = S("sxt")
        sem_wb = [S("sw0"), S("sw1")]
        sem_mm = S("smm")
        sem_ev = S("sev")
        sem_evs = S("sevs")
        sem_st = S("sst")
        sem_qst = S("sqst")
        sem_vst = S("svst")
        sem_dbg = S("sdbg")
        sem_qh = S("sqh")
        sem_kg = S("skg")
        sem_vg = S("svg")
        sem_ast = S("sast")
        sem_cc = S("scc")
        sem_af = S("saf")
        sem_wob = [S("swo0"), S("swo1")]
        sem_xsb = [S("sxs0"), S("sxs1")]
        sem_xrs = S("sxrs")
        sem_x2b = [S("sx20"), S("sx21")]
        sem_ln2b = [S("sl20"), S("sl21")]
        sem_mstb = [S("smst0"), S("smst1")]
        sem_hf = S("shf")
        sem_qw = S("sqw")
        sem_lh = S("slh")
        sem_bd = S("sbd")
        sem_wsh = S("swsh")
        sem_ccw = S("sccw")
        sem_g8 = S("sg8")
        sem_trw = S("strw")
        sem_x2t = S("sx2t")
        sem_wf = [S("swf0"), S("swf1")]
        sem_xs2 = [S("sxs20"), S("sxs21")]
        sem_out = S("sout")
        sem_mrow = S("smrow")
        sem_l4 = S("sl4")
        sem_qb = [S(f"sqb{i}") for i in range(4)]
        sem_l8b = [S("sl8b0"), S("sl8b1")]
        sem_pe2 = S("spe2")
        sem_c = S("sc")
        sem_cv = S("scv")
        sem_vch = S("svch")
        sem_ach = S("sach")
        sem_wbs = [S("swbs0"), S("swbs1")]

        cl = [(0, 1.0 / D), (1, 1e-5), (2, ESHIFT), (3, RS),
              (4, 1.0), (5, 0.0), (6, -1.0), (7, 64.0), (8, -0.5)]
        for i, (col, val) in enumerate(cl):
            last = (i == len(cl) - 1)
            em("vector", lambda e, c=col, v=val, last=last: (
                e.memset(consts[:, c:c + 1], v).then_inc(sem_cv.h, 1) if last
                else e.memset(consts[:, c:c + 1], v)))
        sem_cv.take()
        em("sync", lambda e: e.dma_start(out=perm_t[:], in_=perm_in).then_inc(sem_c.h, 16))
        em("sync", lambda e: e.dma_start(out=ident_t[:], in_=ident_in).then_inc(sem_c.h, 16))
        em("sync", lambda e: e.dma_start(out=ident32_t[:], in_=ident32_in).then_inc(sem_c.h, 16))
        em("sync", lambda e: e.dma_start(out=mod16_t[:], in_=mod16_in).then_inc(sem_c.h, 16))
        sem_c.take(64)
        C_READY = sem_c.n


        xlnT = AR.at(0, 32 * TB, BF16)
        wstage = [AR.at(65536 + i * 8192, 32 * 128, BF16) for i in range(2)]
        ropeCb = AR.at(81920, TB, F32)
        ropeSb = AR.at(86016, TB, F32)
        qraw = AR.at(90112, TB, BF16)
        t1 = AR.at(92160, TB, BF16)
        t2 = AR.at(94208, TB, BF16)
        qrope = AR.at(96256, TB, BF16)
        vstage = AR.at(98304, TB, BF16)
        vtok = AR.at(100352, TB, BF16)
        qwstage = AR.at(102400, 8 * 128, I32)
        dq_tmp = AR.at(106496, D, I32)
        dq_tmp3 = AR.at(122880, D, I32)
        dq_acc = AR.at(139264, D, I32)
        lo_bf = AR.at(106496, D, BF16)
        hi_bf = AR.at(114688, D, BF16)
        lorep_t = [AR.at(102400 + i * 2048, 1024, BF16) for i in range(4)]
        hirep_t = [AR.at(110592 + i * 2048, 1024, BF16) for i in range(4)]
        l8bd_t = [AR.at(118784 + i * 256, 128, BF16) for i in range(2)]
        elo_t = [AR.at(119296 + i * 1024, 512, BF16) for i in range(4)]
        ehi_t = [AR.at(123392 + i * 1024, 512, BF16) for i in range(4)]
        e4_t2 = [AR.at(127488 + i * 1024, 512, BF16) for i in range(4)]
        p2_t = [AR.at(131584 + i * 1024, 512, BF16) for i in range(2)]
        w8row = AR.at(133632, D, BF16)
        w4row = AR.at(141824, D, BF16)
        tr_in = AR.at(102400, D, BF16)
        tr_out = AR.at(110592, D, BF16)
        xtile = [AR.at(155648 + i * 16384, D, F32) for i in range(2)]
        xlnt = [AR.at(188416, D, BF16)]
        zero_bf = AR.at(155648, D, BF16)

        def rawap(handle, dims, offset=0):
            return bass.AP(handle, offset, [list(d) for d in dims])

        em("sync", lambda e: e.dma_start(
            out=repl_t[0:8, :], in_=repl_in).then_inc(sem_c.h, 16))
        em("sync", lambda e: e.dma_start(
            out=sumones_t[:, :], in_=sumones_in).then_inc(sem_c.h, 16))
        sem_c.take(32)
        DQC = sem_c.n
        C_READY = DQC

        em("vector", lambda e: e.memset(zero_bf, 0.0).then_inc(sem_vch.h, 1))
        z_done = sem_vch.take()
        wait("gpsimd", sem_vch, z_done)
        for z in range(2):
            em("gpsimd", lambda e, z=z: e.dma_start(
                out=rawap(l8bd_d, [[128, 128], [16384, 32], [1, 128]],
                          offset=z * 32 * 16384),
                in_=zero_bf.rearrange("p (g c) -> p g c", c=128)).then_inc(sem_bd.h, 16))
            sem_bd.take(16)
        wait("gpsimd", sem_bd, sem_bd.n)
        for o8 in range(8):
            em("gpsimd", lambda e, o8=o8: e.dma_start(
                out=rawap(l8bd_d, [[16384, NG], [128, 16], [1, 16]], offset=o8 * 2064),
                in_=rawap(lut8_in.tensor, [[8 * 256, NG], [16, 16], [1, 16]],
                          offset=o8 * 256)).then_inc(sem_bd.h, 16))
            sem_bd.take(16)
        BD_DONE = sem_bd.n
        em("sync", lambda e: e.dma_start(
            out=lut4g_t[:, :], in_=lut4_in).then_inc(sem_l4.h, 16))
        l4_ld = sem_l4.take(16)
        wait("tensor", sem_l4, l4_ld)
        wait("tensor", sem_c, DQC)
        em("tensor", lambda e: e.transpose(
            p_d[:, 0:64], lut4g_t[:, :], ident32_t[0:64, 0:64]).then_inc(sem_pe2.h, 1))
        l4_tr = sem_pe2.take()
        wait("scalar", sem_pe2, l4_tr)
        em("scalar", lambda e: e.copy(lut4bc[:, :], p_d[:, 0:64]).then_inc(sem_ach.h, 1))
        L4BC = sem_ach.take()
        if dbg_l4bc is not None:
            wait("gpsimd", sem_ach, L4BC)
            em("gpsimd", lambda e: e.dma_start(
                out=dbg_l4bc, in_=lut4bc[:, :]).then_inc(sem_dbg.h, 16))
            sem_dbg.take(16)

        xln_written = []
        norm_done = []
        ZD = z_done

        def ln_gen():
            for t in range(32):
                buf = xtile[t % 2]
                sq = xlnt[0]
                sc = (t % 2) * 8
                if t >= 2:
                    wait("sync", sem_ev, norm_done[t - 2])
                elif t == 0:
                    wait("sync", sem_bd, 32)
                em("sync", lambda e, t=t, b=buf: e.dma_start(
                    out=b, in_=x_in[t * 128:(t + 1) * 128, :]).then_inc(sem_xb[t % 2].h, 16))
                x_ld = sem_xb[t % 2].take(16)
                wait("scalar", sem_xb[t % 2], x_ld)
                wait("scalar", sem_cv, 1)
                if t >= 1:
                    wait("scalar", sem_ev, norm_done[t - 1])
                    wait("scalar", sem_lnb[(t - 1) % 2], xln_written[t - 1])
                em("scalar", lambda e, b=buf, s=sq, sc=sc: e.activation(
                    s, b, AF.Square,
                    accum_out=stats[:, sc + 1:sc + 2]).then_inc(sem_ach.h, 1))
                sq_done = sem_ach.take()
                wait("vector", sem_xb[t % 2], x_ld)
                em("vector", lambda e, b=buf, sc=sc: e.reduce_sum(
                    stats[:, sc:sc + 1], b, axis=AX.X).then_inc(sem_vch.h, 1))
                v_sum = sem_vch.take()
                wait("vector", sem_ach, sq_done)
                em("vector", lambda e, sc=sc: e.tensor_scalar(
                    stats[:, sc + 3:sc + 4], stats[:, sc + 1:sc + 2], consts[:, 0:1],
                    None, ALU.mult).then_inc(sem_vch.h, 1))
                sem_vch.take()
                wait("vector", sem_vch, v_sum)
                em("vector", lambda e, sc=sc: e.tensor_scalar(
                    stats[:, sc + 2:sc + 3], stats[:, sc:sc + 1], consts[:, 0:1],
                    None, ALU.mult).then_inc(sem_vch.h, 1))
                v_mean = sem_vch.take()
                wait("vector", sem_vch, v_mean)
                em("vector", lambda e, sc=sc: e.scalar_tensor_tensor(
                    stats[:, sc + 4:sc + 5], stats[:, sc + 2:sc + 3], stats[:, sc + 2:sc + 3],
                    stats[:, sc + 3:sc + 4], ALU.mult, ALU.subtract).then_inc(sem_vch.h, 1))
                v_nv = sem_vch.take()
                wait("scalar", sem_vch, v_nv)
                em("scalar", lambda e, sc=sc: e.activation(
                    stats[:, sc + 5:sc + 6], stats[:, sc + 4:sc + 5], AF.Ln,
                    bias=consts[:, 1:2], scale=consts[:, 6:7]).then_inc(sem_ach.h, 1))
                a_ln = sem_ach.take()
                wait("scalar", sem_ach, a_ln)
                em("scalar", lambda e, sc=sc: e.activation(
                    stats[:, sc + 5:sc + 6], stats[:, sc + 5:sc + 6], AF.Exp,
                    scale=consts[:, 8:9]).then_inc(sem_ach.h, 1))
                a_rstd = sem_ach.take()
                wait("vector", sem_ach, a_rstd)
                if t >= 1:
                    wait("vector", sem_lnb[(t - 1) % 2], xln_written[t - 1])
                em("vector", lambda e, b=buf, sc=sc, xo=sq: e.tensor_scalar(
                    xo, b, stats[:, sc + 2:sc + 3], stats[:, sc + 5:sc + 6],
                    ALU.subtract, ALU.mult).then_inc(sem_ev.h, 1))
                norm_done.append(sem_ev.take())
                wait("gpsimd", sem_ev, norm_done[-1])
                em("gpsimd", lambda e, t=t, xo=sq: e.dma_start(
                    out=xln_d[t * 128:(t + 1) * 128, :], in_=xo).then_inc(sem_lnb[t % 2].h, 16))
                xln_written.append(sem_lnb[t % 2].take(16))
                yield

        def ext_gen():
            qw_consumed = 0
            lohi = []
            for ot in range(4):
                if qw_consumed:
                    wait("sync", sem_vch, qw_consumed)
                em("sync", lambda e, ot=ot: e.dma_start(
                    out=qwstage.rearrange("p (pl w) -> p pl w", pl=8),
                    in_=rawap(qw_in.tensor, [[128, 128], [OS * 128, 8], [1, 128]],
                              offset=ot * 128 * 128)).then_inc(sem_qw.h, 16))
                qw_ld = sem_qw.take(16)
                wait("vector", sem_qw, qw_ld)
                prev_stt = 0
                for p8 in range(8):
                    for s in range(32):
                        if prev_stt:
                            wait("vector", sem_vch, prev_stt)
                        em("vector", lambda e, p8=p8, s=s: e.tensor_scalar(
                            dq_tmp[:, s::32], qwstage[:, p8 * 128:(p8 + 1) * 128], s, 1,
                            ALU.logical_shift_right, ALU.bitwise_and).then_inc(sem_vch.h, 1))
                        sem_vch.take()
                    ex_done = sem_vch.n
                    wait("vector", sem_vch, ex_done)
                    if p8 == 0:
                        em("vector", lambda e: e.tensor_copy(
                            dq_acc, dq_tmp).then_inc(sem_vch.h, 1))
                        prev_stt = sem_vch.take()
                    else:
                        em("vector", lambda e, p8=p8: e.tensor_scalar(
                            dq_tmp3, dq_tmp, p8, None,
                            ALU.logical_shift_left).then_inc(sem_vch.h, 1))
                        v_sh = sem_vch.take()
                        wait("vector", sem_vch, v_sh)
                        em("vector", lambda e: e.tensor_tensor(
                            dq_acc, dq_acc, dq_tmp3, ALU.bitwise_or).then_inc(sem_vch.h, 1))
                        prev_stt = sem_vch.take()
                    yield
                wait("vector", sem_vch, prev_stt)
                em("vector", lambda e: e.tensor_scalar(
                    dq_tmp3, dq_acc, 15, None, ALU.bitwise_and).then_inc(sem_vch.h, 1))
                v1 = sem_vch.take()
                wait("vector", sem_vch, v1)
                em("vector", lambda e: e.tensor_copy(lo_bf, dq_tmp3).then_inc(sem_vch.h, 1))
                v2 = sem_vch.take()
                wait("vector", sem_vch, v2)
                em("vector", lambda e: e.tensor_scalar(
                    dq_tmp3, dq_acc, 4, None, ALU.logical_shift_right).then_inc(sem_vch.h, 1))
                v3 = sem_vch.take()
                wait("vector", sem_vch, v3)
                em("vector", lambda e: e.tensor_copy(hi_bf, dq_tmp3).then_inc(sem_vch.h, 1))
                v4 = sem_vch.take()
                qw_consumed = v4
                wait("gpsimd", sem_vch, v4)
                for rep in range(16):
                    em("gpsimd", lambda e, ot=ot, rep=rep: e.dma_start(
                        out=rawap(lorep_d, [[16 * D, 128], [1, D]],
                                  offset=(128 * ot * 16 + rep) * D),
                        in_=lo_bf).then_inc(sem_lh.h, 16))
                    em("gpsimd", lambda e, ot=ot, rep=rep: e.dma_start(
                        out=rawap(hirep_d, [[16 * D, 128], [1, D]],
                                  offset=(128 * ot * 16 + rep) * D),
                        in_=hi_bf).then_inc(sem_lh.h, 16))
                    lohi.append(sem_lh.take(32))
                if ot < 3:
                    wait("vector", sem_lh, lohi[-1])
                yield

        EXT_LH = [0, 0]

        def deq_gen():
            wsh_written = []
            eq_done = {}
            e4_done = {}
            s1_mm = {}
            s3_mm = {}
            p2_done = {}
            w_ev = {}
            q_ld = {}
            l8_ld = {}

            def load_quarter(qq):
                if qq >= 256:
                    return
                g, qi = qq // 4, qq % 4
                bb = qq % 4
                if qq >= 4 and (qq - 4) * 2 + 1 in eq_done:
                    wait("sync", sem_vch, e4_done[(qq - 4) * 2 + 1])
                em("sync", lambda e, g=g, qi=qi, bb=bb: e.dma_start(
                    out=lorep_t[bb],
                    in_=lorep_d[g * 128:(g + 1) * 128, qi * 1024:(qi + 1) * 1024]
                ).then_inc(sem_qb[bb].h, 16))
                em("sync", lambda e, g=g, qi=qi, bb=bb: e.dma_start(
                    out=hirep_t[bb],
                    in_=hirep_d[g * 128:(g + 1) * 128, qi * 1024:(qi + 1) * 1024]
                ).then_inc(sem_qb[bb].h, 16))
                q_ld[qq] = sem_qb[bb].take(32)

            def load_l8bd(g):
                if g >= NG:
                    return
                if g >= 2 and (g - 2, 7) in s1_mm:
                    wait("sync", sem_mm, s1_mm[(g - 2, 7)])
                em("sync", lambda e, g=g: e.dma_start(
                    out=l8bd_t[g % 2], in_=l8bd_d[g, :, :]).then_inc(sem_l8b[g % 2].h, 16))
                l8_ld[g] = sem_l8b[g % 2].take(16)

            wait("sync", sem_bd, BD_DONE)
            wait("sync", sem_lh, EXT_LH[0])
            wait("sync", sem_vch, EXT_LH[1])
            for qq in range(4):
                load_quarter(qq)
            load_l8bd(0)
            load_l8bd(1)

            def emit_s3(kp):
                g, ch = kp // 8, kp % 8
                b2 = kp % 2
                k = (g, ch)
                wait("tensor", sem_vch, p2_done[k])
                if kp >= 2:
                    wait("tensor", sem_ach, wev_flat[kp - 2])
                em("tensor", lambda e, b2=b2: e.matmul(
                    p_b[0:8, b2 * 512:(b2 + 1) * 512], sumones_t[:], p2_t[b2],
                    start=True, stop=True))
                em("tensor", lambda e, b2=b2, kp=kp: e.matmul(
                    p_b[32:40, b2 * 512:(b2 + 1) * 512], sumones_t[:], e4_t2[kp % 4],
                    start=True, stop=True,
                    tile_position=(0, 32)).then_inc(sem_mm.h, 1))
                s3_mm[k] = sem_mm.take()
                s3m_flat.append(s3_mm[k])
                last_s3_box[0] = s3_mm[k]
                cs = slice(ch * 512, (ch + 1) * 512)
                wait("scalar", sem_mm, s3_mm[k])
                if ch == 0 and wsh_written:
                    wait("scalar", sem_wsh, wsh_written[-1])
                em("scalar", lambda e, cs=cs, b2=b2: e.copy(
                    w8row[0:8, cs], p_b[0:8, b2 * 512:(b2 + 1) * 512]).then_inc(sem_ach.h, 1))
                sem_ach.take()
                em("scalar", lambda e, cs=cs, b2=b2: e.copy(
                    w4row[0:8, cs], p_b[32:40, b2 * 512:(b2 + 1) * 512]).then_inc(sem_ach.h, 1))
                w_ev[k] = sem_ach.take()
                wev_flat.append(w_ev[k])
                last_w_ev_box[0] = w_ev[k]
                if ch == 7:
                    load_l8bd(g + 2)
                    wait("gpsimd", sem_ach, w_ev[k])
                    em("gpsimd", lambda e, g=g: e.dma_start(
                        out=wsh_d[8 * g:8 * g + 8, 0:D],
                        in_=w8row[0:8, :]).then_inc(sem_wsh.h, 16))
                    em("gpsimd", lambda e, g=g: e.dma_start(
                        out=wsh_d[8 * g:8 * g + 8, D:2 * D],
                        in_=w4row[0:8, :]).then_inc(sem_wsh.h, 16))
                    wsh_written.append(sem_wsh.take(32))

            last_s3_box = [0]
            last_w_ev_box = [0]

            for kk in range(512):
                g, ch = kk // 8, kk % 8
                if dbg_deq is not None and kk == 2:
                    wait("tensor", sem_dbg, 48)
                    wait("vector", sem_dbg, 48)
                    wait("sync", sem_dbg, 48)
                qq = kk // 2
                coff = (kk % 2) * 512
                b4 = kk % 4
                b2 = kk % 2
                k = (g, ch)
                first = (kk == 0)
                wait("vector", sem_qb[qq % 4], q_ld[qq])
                if kk >= 4:
                    wait("vector", sem_mm, s1_mm_flat[kk - 4])
                    wait("vector", sem_vch, p2d_flat[kk - 4])
                em("vector", lambda e, b4=b4, qb=qq % 4, coff=coff: e.tensor_scalar(
                    elo_t[b4], lorep_t[qb][:, coff:coff + 512], mod16_t[:, 0:1], None,
                    ALU.is_equal).then_inc(sem_vch.h, 1))
                sem_vch.take()
                em("vector", lambda e, b4=b4, qb=qq % 4, coff=coff: e.tensor_scalar(
                    ehi_t[b4], hirep_t[qb][:, coff:coff + 512], mod16_t[:, 0:1], None,
                    ALU.is_equal).then_inc(sem_vch.h, 1))
                eq_done[kk] = sem_vch.take()
                wait("vector", sem_ach, L4BC)
                wait("vector", sem_vch, eq_done[kk])
                em("vector", lambda e, b4=b4, g=g: e.tensor_scalar(
                    e4_t2[b4], elo_t[b4], lut4bc[:, g:g + 1], None,
                    ALU.mult).then_inc(sem_vch.h, 1))
                e4_done[kk] = sem_vch.take()
                if kk % 2 == 1:
                    load_quarter(qq + 4)
                wait("tensor", sem_vch, e4_done[kk])
                if ch == 0:
                    wait("tensor", sem_l8b[g % 2], l8_ld[g])
                if kk == 0:
                    wait("tensor", sem_c, DQC)
                    wait("tensor", sem_ach, L4BC)
                if kk >= 2:
                    wait("tensor", sem_vch, p2d_flat[kk - 2])
                em("tensor", lambda e, b2=b2, g=g, kk=kk: e.matmul(
                    p_d[:, b2 * 512:(b2 + 1) * 512], l8bd_t[g % 2], elo_t[kk % 4],
                    start=True, stop=True).then_inc(sem_mm.h, 1))
                s1_mm[k] = sem_mm.take()
                s1_mm_flat.append(s1_mm[k])
                wait("vector", sem_mm, s1_mm[k])
                if kk >= 2:
                    wait("vector", sem_mm, s3m_flat[kk - 2])
                em("vector", lambda e, b2=b2, kk=kk: e.tensor_tensor(
                    p2_t[b2], p_d[:, b2 * 512:(b2 + 1) * 512], ehi_t[kk % 4],
                    ALU.mult).then_inc(sem_vch.h, 1))
                p2_done[k] = sem_vch.take()
                p2d_flat.append(p2_done[k])
                if dbg_deq is not None and kk == 0:
                    wait("gpsimd", sem_vch, p2_done[k])
                    em("gpsimd", lambda e: e.dma_start(
                        out=dbg_deq[:, 0:512], in_=elo_t[0]).then_inc(sem_dbg.h, 16))
                    em("gpsimd", lambda e: e.dma_start(
                        out=dbg_deq[:, 512:1024], in_=ehi_t[0]).then_inc(sem_dbg.h, 16))
                    em("gpsimd", lambda e: e.dma_start(
                        out=dbg_deq[:, 1024:1536], in_=p2_t[0]).then_inc(sem_dbg.h, 16))
                    sem_dbg.take(48)

                if kk >= 1:
                    emit_s3(kk - 1)
                yield
            emit_s3(511)
            tr_ld_prev = 0
            tr_st_prev = 0
            tr_ev = 0
            for wi in range(2):
                for ot in range(4):
                    wait("sync", sem_wsh, wsh_written[-1])
                    wait("sync", sem_mm, last_s3_box[0])
                    wait("sync", sem_vch, p2d_flat[-1])
                    if tr_st_prev:
                        wait("sync", sem_trw, tr_st_prev)
                    em("sync", lambda e, wi=wi, ot=ot: e.dma_start(
                        out=tr_in, in_=wsh_d[ot * 128:(ot + 1) * 128, wi * D:(wi + 1) * D]
                    ).then_inc(sem_qw.h, 16))
                    tr_ld = sem_qw.take(16)
                    wait("tensor", sem_qw, tr_ld)
                    if wi == 0 and ot == 0:
                        wait("tensor", sem_ach, last_w_ev_box[0])
                    if tr_ld_prev:
                        wait("tensor", sem_ach, tr_ev)
                    for ic in range(32):
                        em("tensor", lambda e, ic=ic: e.transpose(
                            p_d[:].bitcast(BF16)[:, (ic % 16) * 128:(ic % 16 + 1) * 128],
                            tr_in[:, ic * 128:(ic + 1) * 128], ident_t[:]
                        ).then_inc(sem_pe2.h, 1) if ic % 16 == 15 else e.transpose(
                            p_d[:].bitcast(BF16)[:, (ic % 16) * 128:(ic % 16 + 1) * 128],
                            tr_in[:, ic * 128:(ic + 1) * 128], ident_t[:]))
                        if ic % 16 == 15:
                            half_tr = sem_pe2.take()
                            wait("scalar", sem_pe2, half_tr)
                            h0 = (ic // 16) * 16 * 128
                            if tr_st_prev:
                                wait("scalar", sem_trw, tr_st_prev)
                            em("scalar", lambda e, h0=h0: e.copy(
                                tr_out[:, h0:h0 + 2048],
                                p_d[:].bitcast(BF16)[:, 0:2048]).then_inc(sem_ach.h, 1))
                            tr_ev = sem_ach.take()
                            wait("tensor", sem_ach, tr_ev)
                    tr_ld_prev = tr_ld
                    wait("gpsimd", sem_ach, tr_ev)
                    em("gpsimd", lambda e, wi=wi, ot=ot: e.dma_start(
                        out=rawap(wshT_d, [[1024, 128], [128 * 1024, 32], [1, 128]],
                                  offset=wi * 512 + ot * 128),
                        in_=tr_out.rearrange("p (ic c) -> p ic c", c=128)).then_inc(sem_trw.h, 16))
                    tr_st_prev = sem_trw.take(16)
                    yield
            wait("gpsimd", sem_trw, sem_trw.n)
            em("gpsimd", lambda e: e.collective_compute(
                "AllGather", ALU.bypass, replica_groups=[list(range(8))],
                ins=[wshT_d.ap().opt()], outs=[wfull_d.ap().opt()]).then_inc(sem_ccw.h, 1))
            sem_ccw.take()
            yield

        s1_mm_flat = []
        p2d_flat = []
        s3m_flat = []
        wev_flat = []

        CCW_BOX = [0]

        def phaseA_gen():
            psum_free = [0, 0]
            psw_free = 0
            qraw_free = 0
            qraw_free_pe2 = 0
            vstage_free = 0
            stage_free = [0, 0]
            wbs_val = [0, 0]
            blk_mm_done = 0
            blk_pe2_done = 0
            pre_w = {}
            for ci0 in range(2):
                for s4 in range(4):
                    em("gpsimd", lambda e, j=ci0, sg=ci0, s4=s4,
                       sh=sem_wb[ci0].h: e.dma_start(
                        out=wstage[sg][:, s4 * 1024:(s4 + 1) * 1024].rearrange(
                            "p (c f) -> p c f", c=8),
                        in_=wq_in[s4 * 1024:(s4 + 1) * 1024,
                                  j * 128:(j + 1) * 128].rearrange(
                            "(c p) f -> p c f", p=128)).then_inc(sh, 16))
                pre_w[ci0] = sem_wb[ci0].take(64)
            yield
            p_qk = [p_a, p_a]
            p_sw = p_c
            qr_store = 0
            vt_store = 0
            for blk in range(NB):
                tok0 = blk * TB
                for tt in range(8 * blk, 8 * blk + 8):
                    wait("sync", sem_lnb[tt % 2], xln_written[tt])
                if blk > 0:
                    wait("sync", sem_mm, blk_mm_done)
                    wait("sync", sem_pe2, blk_pe2_done)
                for dc in range(32):
                    em("sync", lambda e, dc=dc, tok0=tok0: e.dma_start(
                        out=xlnT[:, dc * TB:(dc + 1) * TB],
                        in_=xln_d[tok0:tok0 + TB, dc * 128:(dc + 1) * 128],
                        transpose=True).then_inc(sem_xt.h, 16))
                    sem_xt.take(16)
                em("sync", lambda e, tok0=tok0: e.dma_start(
                    out=ropeCb, in_=ropeC_in[:, tok0:tok0 + TB]).then_inc(sem_xt.h, 16))
                em("sync", lambda e, tok0=tok0: e.dma_start(
                    out=ropeSb, in_=ropeS_in[:, tok0:tok0 + TB]).then_inc(sem_xt.h, 16))
                sem_xt.take(32)
                xt_done = sem_xt.n
                yield

                chunks = [("q", i) for i in range(16)] + [("k", i) for i in range(4)] \
                    + [("v", i) for i in range(4)]
                for ci, (kind, idx) in enumerate(chunks):
                    pg = 0
                    sg = ci % 2
                    win = {"q": wq_in, "k": wk_in, "v": wv_in}[kind]
                    if blk == 0 and ci < 2:
                        w_ld = pre_w[ci]
                        wait("gpsimd", sem_wb[sg], w_ld)
                        em("gpsimd", lambda e, ci=ci, sg=sg: e.dma_start(
                            out=wbf_d[ci * 128:(ci + 1) * 128, :],
                            in_=wstage[sg]).then_inc(sem_wbs[sg].h, 16))
                        wbs_val[sg] = sem_wbs[sg].take(16)
                    elif blk == 0:
                        wait("gpsimd", sem_mm, stage_free[sg])
                        wait("gpsimd", sem_wbs[sg], wbs_val[sg])
                        for s4 in range(4):
                            em("gpsimd", lambda e, w=win, j=idx, sg=sg, s4=s4: e.dma_start(
                                out=wstage[sg][:, s4 * 1024:(s4 + 1) * 1024].rearrange(
                                    "p (c f) -> p c f", c=8),
                                in_=w[s4 * 1024:(s4 + 1) * 1024,
                                      j * 128:(j + 1) * 128].rearrange(
                                    "(c p) f -> p c f", p=128)
                            ).then_inc(sem_wb[sg].h, 16))
                        w_ld = sem_wb[sg].take(64)
                        wait("gpsimd", sem_wb[sg], w_ld)
                        em("gpsimd", lambda e, ci=ci, sg=sg: e.dma_start(
                            out=wbf_d[ci * 128:(ci + 1) * 128, :],
                            in_=wstage[sg]).then_inc(sem_wbs[sg].h, 16))
                        wbs_val[sg] = sem_wbs[sg].take(16)
                    else:
                        wait("sync", sem_mm, stage_free[sg])
                        wait("sync", sem_wbs[sg], wbs_val[sg])
                        for s2 in range(2):
                            em("sync", lambda e, ci=ci, sg=sg, s2=s2: e.dma_start(
                                out=wstage[sg][:, s2 * 2048:(s2 + 1) * 2048],
                                in_=wbf_d[ci * 128:(ci + 1) * 128,
                                          s2 * 2048:(s2 + 1) * 2048]
                            ).then_inc(sem_wb[sg].h, 16))
                        w_ld = sem_wb[sg].take(32)
                    wait("tensor", sem_wb[sg], w_ld)
                    wait("tensor", sem_xt, xt_done)
                    wait("tensor", sem_evs, psum_free[pg])
                    for d in range(32):
                        for j in range(2):
                            last = (d == 31 and j == 1)
                            em("tensor", lambda e, d=d, j=j, pg=pg, sg=sg, last=last: (
                                e.matmul(
                                    p_qk[pg][:, j * 512:(j + 1) * 512],
                                    wstage[sg][:, d * 128:(d + 1) * 128],
                                    xlnT[:, d * TB + j * 512:d * TB + (j + 1) * 512],
                                    start=(d == 0), stop=(d == 31)
                                ).then_inc(sem_mm.h, 1) if last else e.matmul(
                                    p_qk[pg][:, j * 512:(j + 1) * 512],
                                    wstage[sg][:, d * 128:(d + 1) * 128],
                                    xlnT[:, d * TB + j * 512:d * TB + (j + 1) * 512],
                                    start=(d == 0), stop=(d == 31))))
                    stage_free[sg] = sem_mm.take()
                    mm_done = sem_mm.n

                    if kind in ("q", "k"):
                        prev_qraw_free = qraw_free
                        wait("scalar", sem_mm, mm_done)
                        wait("scalar", sem_pe2, qraw_free_pe2)
                        wait("scalar", sem_ev, qraw_free)
                        em("scalar", lambda e, pg=pg: e.copy(
                            qraw, p_qk[pg][:]).then_inc(sem_evs.h, 1))
                        psum_free[pg] = sem_evs.take()
                        ev_qraw = psum_free[pg]
                        wait("tensor", sem_evs, ev_qraw)
                        wait("tensor", sem_c, C_READY)
                        wait("tensor", sem_ev, psw_free)
                        for j in range(2):
                            em("tensor", lambda e, j=j, last=(j == 1): (
                                e.matmul(
                                    p_sw[:, j * 512:(j + 1) * 512], perm_t[:],
                                    qraw[:, j * 512:(j + 1) * 512], start=True, stop=True
                                ).then_inc(sem_pe2.h, 1) if last else e.matmul(
                                    p_sw[:, j * 512:(j + 1) * 512], perm_t[:],
                                    qraw[:, j * 512:(j + 1) * 512], start=True, stop=True)))
                        qraw_free_pe2 = sem_pe2.take()
                        wait("vector", sem_evs, ev_qraw)
                        wait("vector", sem_xt, xt_done)
                        wait("vector", sem_ev, prev_qraw_free)
                        em("vector", lambda e: e.tensor_tensor(
                            t1, qraw, ropeCb, ALU.mult).then_inc(sem_vch.h, 1))
                        v_t1 = sem_vch.take()
                        wait("vector", sem_pe2, qraw_free_pe2)
                        em("vector", lambda e: e.tensor_tensor(
                            t2, p_sw[:], ropeSb, ALU.mult).then_inc(sem_vch.h, 1))
                        v_t2 = sem_vch.take()
                        if qr_store:
                            wait("vector", sem_qst, qr_store)
                        wait("vector", sem_vch, v_t2)
                        em("vector", lambda e: e.tensor_tensor(
                            qrope, t1, t2, ALU.add).then_inc(sem_ev.h, 1))
                        psw_free = sem_ev.take()
                        qraw_free = psw_free
                        wait("gpsimd", sem_ev, qraw_free)
                        dst = q_d if kind == "q" else k_d
                        em("gpsimd", lambda e, dst=dst, idx=idx, tok0=tok0: e.dma_start(
                            out=dst[idx * 128:(idx + 1) * 128, tok0:tok0 + TB],
                            in_=qrope).then_inc(sem_qst.h, 16))
                        qr_store = sem_qst.take(16)
                    else:
                        wait("scalar", sem_mm, mm_done)
                        wait("scalar", sem_pe2, qraw_free_pe2)
                        wait("scalar", sem_ev, vstage_free)
                        em("scalar", lambda e, pg=pg: e.copy(
                            vstage, p_qk[pg][:]).then_inc(sem_evs.h, 1))
                        psum_free[pg] = sem_evs.take()
                        ev_v = psum_free[pg]
                        wait("tensor", sem_evs, ev_v)
                        wait("tensor", sem_ev, psw_free)
                        for tt in range(TB // 128):
                            em("tensor", lambda e, tt=tt, last=(tt == TB // 128 - 1): (
                                e.transpose(
                                    p_sw[:].bitcast(BF16)[:, tt * 128:(tt + 1) * 128],
                                    vstage[:, tt * 128:(tt + 1) * 128], ident_t[:]
                                ).then_inc(sem_pe2.h, 1) if last else e.transpose(
                                    p_sw[:].bitcast(BF16)[:, tt * 128:(tt + 1) * 128],
                                    vstage[:, tt * 128:(tt + 1) * 128], ident_t[:])))
                        qraw_free_pe2 = sem_pe2.take()
                        wait("vector", sem_pe2, qraw_free_pe2)
                        if vt_store:
                            wait("vector", sem_vst, vt_store)
                        em("vector", lambda e: e.tensor_copy(
                            vtok, p_sw[:].bitcast(BF16)[:, 0:TB]).then_inc(sem_ev.h, 1))
                        psw_free = sem_ev.take()
                        vstage_free = psw_free
                        wait("gpsimd", sem_ev, vstage_free)
                        em("gpsimd", lambda e, idx=idx, tok0=tok0: e.dma_start(
                            out=v_d[tok0:tok0 + TB, idx * 128:(idx + 1) * 128].rearrange(
                                "(tt p) c -> p tt c", p=128),
                            in_=vtok.rearrange("p (tt c) -> p tt c", c=128)
                        ).then_inc(sem_vst.h, 16))
                        vt_store = sem_vst.take(16)
                    yield
                blk_mm_done = sem_mm.n
                blk_pe2_done = sem_pe2.n

        lg = ln_gen()
        eg = ext_gen()
        ag = phaseA_gen()
        next(ag)
        for _ in range(8):
            next(lg)
        next(ag)
        dg = None
        alive_a, alive_l, alive_e, alive_d = True, True, True, False
        while alive_a or alive_l or alive_e or alive_d:
            if alive_a:
                try:
                    next(ag)
                except StopIteration:
                    alive_a = False
            if alive_l:
                try:
                    next(lg)
                except StopIteration:
                    alive_l = False
            if alive_e:
                try:
                    next(eg)
                except StopIteration:
                    alive_e = False
                    EXT_LH[0] = sem_lh.n
                    EXT_LH[1] = sem_vch.n
                    dg = deq_gen()
                    alive_d = True
            if alive_d:
                ndq = 8 if alive_a else 1000000
                for _ in range(ndq):
                    try:
                        next(dg)
                    except StopIteration:
                        alive_d = False
                        break
        CCW = sem_ccw.n

        AR.reset(0)
        qh_t = [AR.alloc(T, BF16) for _ in range(2)]
        kg_t = [AR.alloc(T, BF16) for _ in range(2)]
        vg_t = [AR.alloc(T, BF16) for _ in range(2)]
        pt_t = [AR.alloc(1024, BF16) for _ in range(4)]
        ptsum_t = AR.alloc(1024, F32)
        rb_t = AR.alloc(1024, F32)
        asb_t = AR.alloc(1024, BF16)
        den_sb = AR.alloc(1024, F32)
        dm_t = AR.alloc(8 * 1024, BF16)
        onesr_t = AR.alloc(128, F32)

        p_st = [p_a, p_b]
        p_at2 = [p_c, p_d]

        A_EV = sem_ev.n
        A_EVS = sem_evs.n
        A_MM = sem_mm.n
        A_PE2 = sem_pe2.n
        A_QST = sem_qst.n
        A_VST = sem_vst.n
        A_ACH = sem_ach.n
        A_VCH = sem_vch.n
        for eng in ("sync", "tensor", "vector", "scalar", "gpsimd"):
            wait(eng, sem_ev, A_EV)
            wait(eng, sem_evs, A_EVS)
            wait(eng, sem_mm, A_MM)
            wait(eng, sem_pe2, A_PE2)
            wait(eng, sem_qst, A_QST)
            wait(eng, sem_vst, A_VST)
            wait(eng, sem_ach, A_ACH)
            wait(eng, sem_vch, A_VCH)

        em("sync", lambda e: e.dma_start(out=dm_t, in_=dmask_in).then_inc(sem_c.h, 16))
        em("sync", lambda e: e.dma_start(
            out=onesr_t[0:1, :], in_=onesr_in).then_inc(sem_c.h, 16))
        sem_c.take(32)
        C2_READY = sem_c.n

        qh_ld = {}
        kg_ld = {}
        vg_ld = {}

        def load_qh(lh, wait_mm):
            wait("sync", sem_mm, wait_mm)
            em("sync", lambda e, lh=lh: e.dma_start(
                out=qh_t[lh % 2], in_=q_d[lh * 128:(lh + 1) * 128, :]
            ).then_inc(sem_qh.h, 16))
            qh_ld[lh] = sem_qh.take(16)

        def load_kv(gi, wait_mm):
            wait("sync", sem_mm, wait_mm)
            em("sync", lambda e, gi=gi: e.dma_start(
                out=kg_t[gi % 2], in_=k_d[gi * 128:(gi + 1) * 128, :]
            ).then_inc(sem_kg.h, 16))
            kg_ld[gi] = sem_kg.take(16)
            em("sync", lambda e, gi=gi: e.dma_start(
                out=vg_t[gi % 2].rearrange("p (tt c) -> p tt c", c=128),
                in_=v_d[:, gi * 128:(gi + 1) * 128].rearrange(
                    "(tt p) c -> p tt c", p=128)).then_inc(sem_vg.h, 16))
            vg_ld[gi] = sem_vg.take(16)

        load_kv(0, 0)
        load_qh(0, 0)
        load_qh(1, 0)

        segs = [(gi, a, 4 * gi + a, qc)
                for gi in range(NKVL) for a in range(4) for qc in range(4)]
        bank_free = [0, 0]
        slot_av = [0, 0, 0, 0]
        slot_ps = [0, 0, 0, 0]
        av_last = {}
        att_ev_vals = {-1: 0}
        ast_vals = {-1: 0}
        rb_vals = {-1: 0}
        den_vals = {-1: 0}
        tail_prev = None

        def tail_den(P):
            k, gi, a, qc, ptsum_final, av_end = P
            pn = (k + 1) % 2
            wait("tensor", sem_vch, ptsum_final)
            wait("tensor", sem_ev, att_ev_vals[k - 1])
            for j in range(2):
                em("tensor", lambda e, j=j, pn=pn, last=(j == 1): (
                    e.matmul(p_at2[pn][0:1, j * 512:(j + 1) * 512], consts[:, 4:5],
                             ptsum_t[:, j * 512:(j + 1) * 512],
                             start=True, stop=True).then_inc(sem_mm.h, 1) if last
                    else e.matmul(p_at2[pn][0:1, j * 512:(j + 1) * 512],
                                  consts[:, 4:5],
                                  ptsum_t[:, j * 512:(j + 1) * 512],
                                  start=True, stop=True)))
            den_vals[k] = sem_mm.take()

        def tail_recip(P):
            k = P[0]
            pn = (k + 1) % 2
            wait("scalar", sem_mm, den_vals[k])
            em("scalar", lambda e, pn=pn: e.activation(
                den_sb[0:1, :], p_at2[pn][0:1, :], AF.Ln).then_inc(sem_ach.h, 1))
            sem_ach.take()
            em("scalar", lambda e: e.activation(
                den_sb[0:1, :], den_sb[0:1, :], AF.Exp,
                scale=consts[0:1, 6:7]).then_inc(sem_ach.h, 1))
            return sem_ach.take()

        def tail_rest(P, a_rcp):
            k, gi, a, qc, ptsum_final, av_end = P
            pp = k % 2
            pn = (k + 1) % 2
            wait("tensor", sem_ach, a_rcp)
            wait("tensor", sem_c, C2_READY)
            for j in range(2):
                em("tensor", lambda e, j=j, pn=pn, last=(j == 1): (
                    e.matmul(p_at2[pn][:, j * 512:(j + 1) * 512], onesr_t[0:1, :],
                             den_sb[0:1, j * 512:(j + 1) * 512],
                             start=True, stop=True).then_inc(sem_mm.h, 1) if last
                    else e.matmul(p_at2[pn][:, j * 512:(j + 1) * 512],
                                  onesr_t[0:1, :],
                                  den_sb[0:1, j * 512:(j + 1) * 512],
                                  start=True, stop=True)))
            bc_mm = sem_mm.take()
            wait("scalar", sem_mm, bc_mm)
            wait("scalar", sem_ev, att_ev_vals[k - 1])
            em("scalar", lambda e, pn=pn: e.copy(
                rb_t, p_at2[pn][:]).then_inc(sem_ach.h, 1))
            rb_vals[k] = sem_ach.take()
            wait("vector", sem_ach, rb_vals[k])
            wait("vector", sem_mm, av_end)
            wait("vector", sem_ast, ast_vals[k - 1])
            em("vector", lambda e, pp=pp: e.tensor_tensor(
                asb_t, p_at2[pp][:], rb_t, ALU.mult).then_inc(sem_ev.h, 1))
            att_ev_vals[k] = sem_ev.take()
            wait("sync", sem_ev, att_ev_vals[k])
            em("sync", lambda e, gi=gi, a=a, qc=qc: e.dma_start(
                out=att_myg[gi][a * 128:(a + 1) * 128,
                                qc * 1024:(qc + 1) * 1024],
                in_=asb_t).then_inc(sem_ast.h, 16))
            ast_vals[k] = sem_ast.take(16)
            if a == 3 and qc == 3:
                wait("gpsimd", sem_ast, ast_vals[k])
                em("gpsimd", lambda e, gi=gi: e.collective_compute(
                    "AllGather", ALU.bypass,
                    replica_groups=[[0, 1], [2, 3], [4, 5], [6, 7]],
                    ins=[att_myg[gi].ap().opt()],
                    outs=[att_gg[gi].ap().opt()]).then_inc(sem_cc.h, 1))
                sem_cc.take()

        for k, (gi, a, lh, qc) in enumerate(segs):
            nkt = 8 * qc + 8
            if k == 0:
                wait("vector", sem_c, C2_READY)
                wait("scalar", sem_cv, 1)
                wait("tensor", sem_cv, 1)
            if qc == 0:
                if lh + 1 < NQL:
                    load_qh(lh + 1, av_last.get(lh - 1, 0))
                if a == 2 and gi + 1 < NKVL:
                    load_kv(gi + 1, av_last.get(4 * (gi - 1) + 3, 0))
            S_val = {}
            exp_val = {}
            pready = {}

            def emit_S(kt, _lh=lh, _gi=gi, _qc=qc, _k=k):
                wait("tensor", sem_ach, bank_free[kt % 2])
                if _qc == 0 and kt == 0:
                    wait("tensor", sem_qh, qh_ld[_lh])
                    wait("tensor", sem_kg, kg_ld[_gi])
                    wait("tensor", sem_vg, vg_ld[_gi])
                for j in range(2):
                    em("tensor", lambda e, kt=kt, j=j, lh=_lh, gi=_gi, qc=_qc,
                       last=(j == 1): (
                        e.matmul(p_st[kt % 2][:, j * 512:(j + 1) * 512],
                                 kg_t[gi % 2][:, kt * 128:(kt + 1) * 128],
                                 qh_t[lh % 2][:, qc * 1024 + j * 512:
                                              qc * 1024 + (j + 1) * 512],
                                 start=True, stop=True).then_inc(sem_mm.h, 1)
                        if last else
                        e.matmul(p_st[kt % 2][:, j * 512:(j + 1) * 512],
                                 kg_t[gi % 2][:, kt * 128:(kt + 1) * 128],
                                 qh_t[lh % 2][:, qc * 1024 + j * 512:
                                              qc * 1024 + (j + 1) * 512],
                                 start=True, stop=True)))
                S_val[kt] = sem_mm.take()

            def emit_exp(kt):
                wait("scalar", sem_mm, S_val[kt])
                wait("scalar", sem_mm, slot_av[kt % 4])
                wait("scalar", sem_vch, slot_ps[kt % 4])
                em("scalar", lambda e, kt=kt: e.activation(
                    pt_t[kt % 4], p_st[kt % 2][:], AF.Exp, bias=consts[:, 2:3],
                    scale=consts[:, 3:4]).then_inc(sem_ach.h, 1))
                exp_val[kt] = sem_ach.take()
                bank_free[kt % 2] = exp_val[kt]
                pready[kt] = (sem_ach, exp_val[kt])

            def emit_mask(kt, _qc=qc):
                o = kt - 8 * _qc
                wait("vector", sem_ach, exp_val[kt])
                em("vector", lambda e, kt=kt, o=o: e.tensor_tensor(
                    pt_t[kt % 4], pt_t[kt % 4], dm_t[:, o * 1024:(o + 1) * 1024],
                    ALU.mult).then_inc(sem_vch.h, 1))
                pready[kt] = (sem_vch, sem_vch.take())

            P = tail_prev
            if P is not None:
                tail_den(P)
            emit_S(0)
            emit_S(1)
            if P is not None:
                a_rcp = tail_recip(P)
            emit_exp(0)
            emit_exp(1)
            emit_S(2)
            emit_S(3)
            if P is not None:
                tail_rest(P, a_rcp)
            if 0 >= nkt - 8:
                emit_mask(0)
            if 1 >= nkt - 8:
                emit_mask(1)

            av_end = 0
            for kt in range(nkt):
                psem, pval = pready[kt]
                wait("tensor", psem, pval)
                if kt == 0:
                    wait("tensor", sem_ach, rb_vals[k - 1])
                for j in range(2):
                    em("tensor", lambda e, kt=kt, j=j, gi=gi, k=k, nkt=nkt,
                       last=(j == 1): (
                        e.matmul(p_at2[k % 2][:, j * 512:(j + 1) * 512],
                                 vg_t[gi % 2][:, kt * 128:(kt + 1) * 128],
                                 pt_t[kt % 4][:, j * 512:(j + 1) * 512],
                                 start=(kt == 0), stop=(kt == nkt - 1)
                                 ).then_inc(sem_mm.h, 1)
                        if last else
                        e.matmul(p_at2[k % 2][:, j * 512:(j + 1) * 512],
                                 vg_t[gi % 2][:, kt * 128:(kt + 1) * 128],
                                 pt_t[kt % 4][:, j * 512:(j + 1) * 512],
                                 start=(kt == 0), stop=(kt == nkt - 1))))
                avv = sem_mm.take()
                slot_av[kt % 4] = avv
                if kt == nkt - 1:
                    av_end = avv
                if kt + 2 < nkt:
                    emit_exp(kt + 2)
                if kt + 4 < nkt:
                    emit_S(kt + 4)
                if kt == 0:
                    wait("vector", sem_mm, den_vals[k - 1])
                    if kt < nkt - 8:
                        wait("vector", sem_ach, exp_val[kt])
                    em("vector", lambda e: e.tensor_copy(
                        ptsum_t, pt_t[0]).then_inc(sem_vch.h, 1))
                else:
                    if kt < nkt - 8:
                        wait("vector", sem_ach, exp_val[kt])
                    em("vector", lambda e, kt=kt: e.tensor_tensor(
                        ptsum_t, ptsum_t, pt_t[kt % 4],
                        ALU.add).then_inc(sem_vch.h, 1))
                slot_ps[kt % 4] = sem_vch.take()
                if kt + 2 < nkt and kt + 2 >= nkt - 8:
                    emit_mask(kt + 2)
            if qc == 3:
                av_last[lh] = av_end
            tail_prev = (k, gi, a, qc, slot_ps[(nkt - 1) % 4], av_end)

        P = tail_prev
        tail_den(P)
        a_rcp = tail_recip(P)
        tail_rest(P, a_rcp)
        CC_ATT = sem_cc.n

        B_EV = sem_ev.n
        B_MM = sem_mm.n
        B_ACH = sem_ach.n
        B_VCH = sem_vch.n
        B_AST = sem_ast.n
        for eng in ("sync", "tensor", "vector", "scalar", "gpsimd"):
            wait(eng, sem_ev, B_EV)
            wait(eng, sem_mm, B_MM)
            wait(eng, sem_ach, B_ACH)
            wait(eng, sem_vch, B_VCH)
            wait(eng, sem_ast, B_AST)
            wait(eng, sem_cc, CC_ATT)

        AR.reset(0)
        attf = AR.alloc(32 * TH, BF16)
        wost = [AR.alloc(32 * 128, BF16) for _ in range(2)]
        ysb = AR.alloc(TH, F32)
        xstrip = [AR.alloc(TH, F32) for _ in range(2)]
        xres_sb = AR.alloc(TH, F32)

        em("gpsimd", lambda e: e.dma_start(
            out=hoffs_t[0:1, 0:1], in_=hoff_in).then_inc(sem_hf.h, 16))
        sem_hf.take(16)
        wait("gpsimd", sem_hf, sem_hf.n)
        hv_cell = [None]
        em("gpsimd", lambda e: hv_cell.__setitem__(
            0, nc.values_load(hoffs_t[0:1, 0:1], engines=[mybir.EngineType.Pool],
                              skip_runtime_bounds_check=True)))
        for ad in range(32):
            gi4 = (ad % 16) // 4
            roff = (ad // 16) * 512 + (ad % 4) * 128
            em("gpsimd", lambda e, ad=ad, gi4=gi4, roff=roff: e.dma_start(
                out=attf[:, ad * TH:(ad + 1) * TH],
                in_=att_gg[gi4][roff:roff + 128, bass.ds(hv_cell[0], TH)]
            ).then_inc(sem_af.h, 16))
            sem_af.take(16)
        AF_LD = sem_af.n

        ysb_free = 0
        tr_done = 0
        add_done = 0
        xr_store = 0
        xs_ld = [0, 0]
        wo_stage_free = [0, 0]
        xr_stores = []
        xs_consumed = {}
        for cc in range(32):
            sg = cc % 2
            wait("gpsimd", sem_mm, wo_stage_free[sg])
            em("gpsimd", lambda e, cc=cc, sg=sg: e.dma_start(
                out=wost[sg].rearrange("p (c f) -> p c f", c=32),
                in_=wo_in[:, cc * 128:(cc + 1) * 128].rearrange(
                    "(c p) f -> p c f", p=128)).then_inc(sem_wob[sg].h, 16))
            wo_ld = sem_wob[sg].take(16)
            wait("sync", sem_ev, add_done if cc < 2 else xs_consumed[cc - 2])
            em("sync", lambda e, cc=cc, sg=sg: e.dma_start(
                out=xstrip[sg].rearrange("p (tt c) -> p tt c", c=128),
                in_=xh_in[:, cc * 128:(cc + 1) * 128].rearrange(
                    "(tt p) c -> p tt c", p=128)).then_inc(sem_xsb[sg].h, 16))
            xs_ld[sg] = sem_xsb[sg].take(16)
            wait("tensor", sem_wob[sg], wo_ld)
            wait("tensor", sem_af, AF_LD)
            wait("tensor", sem_ach, ysb_free)
            for ad in range(32):
                for j in range(4):
                    pt = [p_a, p_b][j // 2]
                    em("tensor", lambda e, ad=ad, j=j, pt=pt, sg=sg: e.matmul(
                        pt[:, (j % 2) * 512:(j % 2 + 1) * 512],
                        wost[sg][:, ad * 128:(ad + 1) * 128],
                        attf[:, ad * TH + j * 512:ad * TH + (j + 1) * 512],
                        start=(ad == 0), stop=(ad == 31)
                    ).then_inc(sem_mm.h, 1) if (ad == 31 and j == 3) else e.matmul(
                        pt[:, (j % 2) * 512:(j % 2 + 1) * 512],
                        wost[sg][:, ad * 128:(ad + 1) * 128],
                        attf[:, ad * TH + j * 512:ad * TH + (j + 1) * 512],
                        start=(ad == 0), stop=(ad == 31)))
            y_mm = sem_mm.take()
            wo_stage_free[sg] = y_mm
            wait("scalar", sem_mm, y_mm)
            wait("scalar", sem_pe2, tr_done)
            em("scalar", lambda e: e.copy(ysb[:, 0:1024], p_a[:]).then_inc(sem_ach.h, 1))
            sem_ach.take()
            em("scalar", lambda e: e.copy(ysb[:, 1024:2048], p_b[:]).then_inc(sem_ach.h, 1))
            ysb_free = sem_ach.take()
            wait("tensor", sem_ach, ysb_free)
            wait("tensor", sem_ev, add_done)
            for tt in range(16):
                pt = [p_c, p_d][tt // 8]
                em("tensor", lambda e, tt=tt, pt=pt: e.transpose(
                    pt[:, (tt % 8) * 128:(tt % 8 + 1) * 128],
                    ysb[:, tt * 128:(tt + 1) * 128], ident32_t[:]
                ).then_inc(sem_pe2.h, 1) if tt == 15 else e.transpose(
                    pt[:, (tt % 8) * 128:(tt % 8 + 1) * 128],
                    ysb[:, tt * 128:(tt + 1) * 128], ident32_t[:]))
            tr_done = sem_pe2.take()
            wait("vector", sem_pe2, tr_done)
            wait("vector", sem_xsb[sg], xs_ld[sg])
            if xr_stores:
                wait("vector", sem_xrs, xr_stores[-1])
            em("vector", lambda e, sg=sg: e.tensor_tensor(
                xres_sb[:, 0:1024], p_c[:], xstrip[sg][:, 0:1024], ALU.add).then_inc(sem_vch.h, 1))
            sem_vch.take()
            em("vector", lambda e, sg=sg: e.tensor_tensor(
                xres_sb[:, 1024:2048], p_d[:], xstrip[sg][:, 1024:2048],
                ALU.add).then_inc(sem_ev.h, 1))
            add_done = sem_ev.take()
            xs_consumed[cc] = add_done
            wait("sync", sem_ev, add_done)
            em("sync", lambda e, cc=cc: e.dma_start(
                out=xres_d[:, cc * 128:(cc + 1) * 128].rearrange(
                    "(tt p) c -> p tt c", p=128),
                in_=xres_sb.rearrange("p (tt c) -> p tt c", c=128)).then_inc(sem_xrs.h, 16))
            xr_stores.append(sem_xrs.take(16))
        WO_DONE = sem_xrs.n

        C_EV = sem_ev.n
        C_MM = sem_mm.n
        C_ACH = sem_ach.n
        for eng in ("sync", "tensor", "vector", "scalar", "gpsimd"):
            wait(eng, sem_ev, C_EV)
            wait(eng, sem_mm, C_MM)
            wait(eng, sem_ach, C_ACH)
            wait(eng, sem_xrs, WO_DONE)
        AR.reset(65536)
        x2tile = [AR.alloc(D, F32) for _ in range(2)]
        x2ln = [AR.alloc(D, BF16) for _ in range(2)]
        mcol = AR.alloc(16, F32)
        mcol_bf = AR.alloc(16, BF16)
        x2T_pre = AR.at(0, 32 * 1024, BF16)

        ln2_written = []
        norm2_done = []
        mst_vals = []
        for t in range(16):
            buf = x2tile[t % 2]
            xo = x2ln[t % 2]
            sc = (t % 2) * 8
            if t >= 2:
                wait("sync", sem_ev, norm2_done[t - 2])
            for s4 in range(4):
                em("sync", lambda e, t=t, b=buf, s4=s4: e.dma_start(
                    out=b[:, s4 * 1024:(s4 + 1) * 1024],
                    in_=xres_d[t * 128:(t + 1) * 128, s4 * 1024:(s4 + 1) * 1024]
                ).then_inc(sem_x2b[t % 2].h, 16))
            x2_ld = sem_x2b[t % 2].take(64)
            wait("scalar", sem_x2b[t % 2], x2_ld)
            if t >= 2:
                wait("scalar", sem_ev, norm2_done[t - 2])
                wait("scalar", sem_ln2b[t % 2], ln2_written[t - 2])
            em("scalar", lambda e, b=buf, s=xo, sc=sc: e.activation(
                s, b, AF.Square,
                accum_out=stats[:, sc + 1:sc + 2]).then_inc(sem_ach.h, 1))
            sq_done = sem_ach.take()
            wait("vector", sem_x2b[t % 2], x2_ld)
            em("vector", lambda e, b=buf, sc=sc: e.reduce_sum(
                stats[:, sc:sc + 1], b, axis=AX.X).then_inc(sem_vch.h, 1))
            v_sum = sem_vch.take()
            wait("vector", sem_ach, sq_done)
            em("vector", lambda e, sc=sc: e.tensor_scalar(
                stats[:, sc + 3:sc + 4], stats[:, sc + 1:sc + 2], consts[:, 0:1],
                None, ALU.mult).then_inc(sem_vch.h, 1))
            sem_vch.take()
            wait("vector", sem_vch, v_sum)
            em("vector", lambda e, sc=sc: e.tensor_scalar(
                stats[:, sc + 2:sc + 3], stats[:, sc:sc + 1], consts[:, 0:1],
                None, ALU.mult).then_inc(sem_vch.h, 1))
            v_mean = sem_vch.take()
            wait("vector", sem_vch, v_mean)
            em("vector", lambda e, sc=sc: e.scalar_tensor_tensor(
                stats[:, sc + 4:sc + 5], stats[:, sc + 2:sc + 3], stats[:, sc + 2:sc + 3],
                stats[:, sc + 3:sc + 4], ALU.mult, ALU.subtract).then_inc(sem_vch.h, 1))
            v_nv = sem_vch.take()
            wait("vector", sem_vch, v_nv)
            em("vector", lambda e, sc=sc: e.tensor_scalar(
                stats[:, sc + 6:sc + 7], stats[:, sc + 4:sc + 5], consts[:, 6:7],
                consts[:, 1:2], ALU.mult, ALU.add).then_inc(sem_vch.h, 1))
            v_vpe = sem_vch.take()
            wait("vector", sem_vch, v_vpe)
            em("vector", lambda e, sc=sc: e.tensor_scalar(
                stats[:, sc + 7:sc + 8], stats[:, sc + 4:sc + 5], consts[:, 6:7],
                None, ALU.mult).then_inc(sem_vch.h, 1))
            v_var = sem_vch.take()
            wait("vector", sem_vch, v_var)
            if t >= 2:
                wait("vector", sem_mstb[t % 2], mst_vals[t - 2])
            em("vector", lambda e, sc=sc, t=t: e.tensor_tensor(
                mcol_bf[:, t % 2:t % 2 + 1], stats[:, sc + 7:sc + 8],
                stats[:, sc + 6:sc + 7], ALU.is_gt).then_inc(sem_vch.h, 1))
            v_msk = sem_vch.take()
            wait("sync", sem_vch, v_msk)
            em("sync", lambda e, t=t: e.dma_start(
                out=mask_d[t * 128:(t + 1) * 128, :],
                in_=mcol_bf[:, t % 2:t % 2 + 1]).then_inc(sem_mstb[t % 2].h, 16))
            mst_vals.append(sem_mstb[t % 2].take(16))
            wait("scalar", sem_vch, v_nv)
            em("scalar", lambda e, sc=sc: e.activation(
                stats[:, sc + 5:sc + 6], stats[:, sc + 4:sc + 5], AF.Ln,
                bias=consts[:, 1:2], scale=consts[:, 6:7]).then_inc(sem_ach.h, 1))
            a_ln = sem_ach.take()
            wait("scalar", sem_ach, a_ln)
            em("scalar", lambda e, sc=sc: e.activation(
                stats[:, sc + 5:sc + 6], stats[:, sc + 5:sc + 6], AF.Exp,
                scale=consts[:, 8:9]).then_inc(sem_ach.h, 1))
            a_rstd = sem_ach.take()
            wait("vector", sem_ach, a_rstd)
            if t >= 2:
                wait("vector", sem_ln2b[t % 2], ln2_written[t - 2])
            em("vector", lambda e, b=buf, sc=sc, xo=xo: e.tensor_scalar(
                xo, b, stats[:, sc + 2:sc + 3], stats[:, sc + 5:sc + 6],
                ALU.subtract, ALU.mult).then_inc(sem_ev.h, 1))
            norm2_done.append(sem_ev.take())
            wait("gpsimd", sem_ev, norm2_done[-1])
            em("gpsimd", lambda e, t=t, xo=xo: e.dma_start(
                out=xln2_d[t * 128:(t + 1) * 128, :], in_=xo).then_inc(sem_ln2b[t % 2].h, 16))
            ln2_written.append(sem_ln2b[t % 2].take(16))
        LN2_TOTALS = (sem_ln2b[0].n, sem_ln2b[1].n)

        wait("sync", sem_ln2b[0], ln2_written[6])
        wait("sync", sem_ln2b[1], ln2_written[7])
        for dc in range(32):
            em("sync", lambda e, dc=dc: e.dma_start(
                out=x2T_pre[:, dc * 1024:(dc + 1) * 1024],
                in_=xln2_d[0:1024, dc * 128:(dc + 1) * 128],
                transpose=True).then_inc(sem_x2t.h, 16))
            sem_x2t.take(16)

        E_EV = sem_ev.n
        E_VCH = sem_vch.n
        E_ACH = sem_ach.n
        E_MM = sem_mm.n
        E_PE2 = sem_pe2.n
        for eng in ("sync", "tensor", "vector", "scalar", "gpsimd"):
            wait(eng, sem_ev, E_EV)
            wait(eng, sem_vch, E_VCH)
            wait(eng, sem_ach, E_ACH)
            wait(eng, sem_mm, E_MM)
            wait(eng, sem_pe2, E_PE2)
            wait(eng, sem_ccw, CCW)
            wait(eng, sem_ln2b[0], LN2_TOTALS[0])
            wait(eng, sem_ln2b[1], LN2_TOTALS[1])
            wait(eng, sem_mstb[0], sem_mstb[0].n)
            wait(eng, sem_mstb[1], sem_mstb[1].n)
        AR.reset(0)
        x2T = AR.alloc(32 * 1024, BF16)
        wf8 = [AR.alloc(32 * 128, BF16) for _ in range(2)]
        wf4 = [AR.alloc(32 * 128, BF16) for _ in range(2)]
        maskb = AR.alloc(TH, I8)
        mrow = AR.alloc(TH, BF16)
        y8sb = AR.alloc(1024, F32)
        y4sb = AR.alloc(1024, F32)
        ysel = AR.alloc(1024, F32)
        xs2 = [AR.alloc(1024, F32) for _ in range(2)]
        outsb = AR.alloc(1024, F32)
        bias_sb = AR.alloc(32, F32)

        em("sync", lambda e: e.dma_start(
            out=mrow[0:1, :], in_=rawap(mask_d, [[TH, 1], [1, TH]])).then_inc(sem_mrow.h, 16))
        em("sync", lambda e: e.dma_start(
            out=bias_sb, in_=rawap(bias_in.tensor, [[1, 128], [128, 32]]),
            allow_slow_non_contiguous=True).then_inc(sem_mrow.h, 16))
        em("sync", lambda e: e.dma_start(
            out=onesrb_t[0:1, :], in_=onesrb_in).then_inc(sem_mrow.h, 16))
        sem_mrow.take(48)
        MR = sem_mrow.n
        wait("tensor", sem_mrow, MR)
        for j in range(4):
            pt = [p_c, p_d][j // 2]
            em("tensor", lambda e, j=j, pt=pt: e.matmul(
                pt[:, (j % 2) * 512:(j % 2 + 1) * 512], onesrb_t[0:1, :],
                mrow[0:1, j * 512:(j + 1) * 512], start=True, stop=True
            ).then_inc(sem_mm.h, 1) if j == 3 else e.matmul(
                pt[:, (j % 2) * 512:(j % 2 + 1) * 512], onesrb_t[0:1, :],
                mrow[0:1, j * 512:(j + 1) * 512], start=True, stop=True))
        mb_mm = sem_mm.take()
        wait("scalar", sem_mm, mb_mm)
        em("scalar", lambda e: e.copy(maskb[:, 0:1024], p_c[:]).then_inc(sem_ach.h, 1))
        sem_ach.take()
        em("scalar", lambda e: e.copy(maskb[:, 1024:2048], p_d[:]).then_inc(sem_ach.h, 1))
        mb_ev = sem_ach.take()

        out_sel_done = 0
        tr2_done = 0
        add2_done = 0
        y_ev = 0
        wf_free = [0, 0]
        xs2_last = [0, 0]
        out_stores = []
        X2T_HB = [0, 0]
        X2T_HB[0] = sem_x2t.n
        pend = None
        y8_v = 0
        y4_v = 0

        def p8_tail_a(p):
            global_unused = None
            hbp, ocp, sgp, y_mmp, _ = p
            wait("scalar", sem_mm, y_mmp)
            wait("scalar", sem_mrow, MR)
            wait("scalar", sem_vch, out_sel_vals[0])
            em("scalar", lambda e, oc=ocp: e.add(
                y8sb, p_a[:], bias_sb[:, oc:oc + 1]).then_inc(sem_ach.h, 1))
            y8v = sem_ach.take()
            wait("vector", sem_mm, y_mmp)
            em("vector", lambda e, oc=ocp: e.tensor_scalar(
                y4sb, p_b[:], bias_sb[:, oc:oc + 1], None,
                ALU.add).then_inc(sem_evs.h, 1))
            y4v = sem_evs.take()
            wait("vector", sem_ach, mb_ev)
            wait("vector", sem_pe2, tr2_vals[0])
            em("vector", lambda e: e.tensor_copy(ysel, y4sb).then_inc(sem_vch.h, 1))
            sem_vch.take()
            wait("vector", sem_ach, y8v)
            em("vector", lambda e, hb=hbp: e.copy_predicated(
                ysel, maskb[:, hb * 1024:(hb + 1) * 1024],
                y8sb).then_inc(sem_vch.h, 1))
            out_sel_vals[0] = sem_vch.take()
            return y8v, y4v

        def p8_tail_b(p):
            hbp, ocp, sgp, y_mmp, xs2_ldp = p
            wait("tensor", sem_vch, out_sel_vals[0])
            wait("tensor", sem_ev, add2_vals[0])
            wait("tensor", sem_ach, mb_ev)
            for tt in range(8):
                em("tensor", lambda e, tt=tt: e.transpose(
                    p_c[:, tt * 128:(tt + 1) * 128],
                    ysel[:, tt * 128:(tt + 1) * 128], ident32_t[:]
                ).then_inc(sem_pe2.h, 1) if tt == 7 else e.transpose(
                    p_c[:, tt * 128:(tt + 1) * 128],
                    ysel[:, tt * 128:(tt + 1) * 128], ident32_t[:]))
            tr2_vals[0] = sem_pe2.take()
            wait("vector", sem_pe2, tr2_vals[0])
            wait("vector", sem_xs2[sgp], xs2_ldp)
            if out_stores:
                wait("vector", sem_out, out_stores[-1])
            em("vector", lambda e, sg=sgp: e.tensor_tensor(
                outsb, p_c[:], xs2[sg], ALU.add).then_inc(sem_ev.h, 1))
            add2_vals[0] = sem_ev.take()
            xs2_last[sgp] = add2_vals[0]
            wait("sync", sem_ev, add2_vals[0])
            em("sync", lambda e, hb=hbp, oc=ocp: e.dma_start(
                out=rawap(out_ext.tensor, [[D, 128], [128 * D, 8], [1, 128]],
                          offset=hb * 1024 * D + oc * 128),
                in_=outsb.rearrange("p (tt c) -> p tt c", c=128)).then_inc(sem_out.h, 16))
            out_stores.append(sem_out.take(16))

        out_sel_vals = [0]
        tr2_vals = [0]
        add2_vals = [0]

        for tg in range(64):
            hb, oc = tg // 32, tg % 32
            sg = oc % 2
            if hb == 1 and oc == 0:
                wait("sync", sem_mm, sem_mm.n)
                for dc in range(32):
                    em("sync", lambda e, dc=dc: e.dma_start(
                        out=x2T[:, dc * 1024:(dc + 1) * 1024],
                        in_=xln2_d[1024:2048, dc * 128:(dc + 1) * 128],
                        transpose=True).then_inc(sem_x2t.h, 16))
                    sem_x2t.take(16)
                X2T_HB[1] = sem_x2t.n
            wait("sync", sem_mm, wf_free[sg])
            r, wc = oc // 4, (oc % 4) * 128
            for h2 in range(2):
                em("sync", lambda e, r=r, wc=wc, sg=sg, h2=h2: e.dma_start(
                    out=wf8[sg][:, h2 * 2048:(h2 + 1) * 2048].rearrange(
                        "p (c f) -> p c f", c=16),
                    in_=rawap(wfull_d, [[1024, 128], [128 * 1024, 16], [1, 128]],
                              offset=r * D * 1024 + wc + h2 * 16 * 128 * 1024)
                ).then_inc(sem_wf[sg].h, 16))
                em("sync", lambda e, r=r, wc=wc, sg=sg, h2=h2: e.dma_start(
                    out=wf4[sg][:, h2 * 2048:(h2 + 1) * 2048].rearrange(
                        "p (c f) -> p c f", c=16),
                    in_=rawap(wfull_d, [[1024, 128], [128 * 1024, 16], [1, 128]],
                              offset=r * D * 1024 + 512 + wc + h2 * 16 * 128 * 1024)
                ).then_inc(sem_wf[sg].h, 16))
            wf_ld = sem_wf[sg].take(64)
            wait("sync", sem_ev, xs2_last[sg])
            em("sync", lambda e, hb=hb, oc=oc, sg=sg: e.dma_start(
                out=xs2[sg].rearrange("p (tt c) -> p tt c", c=128),
                in_=rawap(xres_d, [[D, 128], [128 * D, 8], [1, 128]],
                          offset=hb * 1024 * D + oc * 128)).then_inc(sem_xs2[sg].h, 16))
            xs2_ld = sem_xs2[sg].take(16)
            if pend is not None:
                y8_v, y4_v = p8_tail_a(pend)
            wait("tensor", sem_wf[sg], wf_ld)
            wait("tensor", sem_x2t, X2T_HB[hb])
            wait("tensor", sem_ach, y8_v)
            wait("tensor", sem_evs, y4_v)
            for dc in range(32):
                for j in range(2):
                    em("tensor", lambda e, dc=dc, j=j, sg=sg: e.matmul(
                        p_a[:, j * 512:(j + 1) * 512],
                        wf8[sg][:, dc * 128:(dc + 1) * 128],
                        x2T[:, dc * 1024 + j * 512:dc * 1024 + (j + 1) * 512],
                        start=(dc == 0), stop=(dc == 31)))
                for j in range(2):
                    em("tensor", lambda e, dc=dc, j=j, sg=sg: e.matmul(
                        p_b[:, j * 512:(j + 1) * 512],
                        wf4[sg][:, dc * 128:(dc + 1) * 128],
                        x2T[:, dc * 1024 + j * 512:dc * 1024 + (j + 1) * 512],
                        start=(dc == 0), stop=(dc == 31)
                    ).then_inc(sem_mm.h, 1) if (dc == 31 and j == 1) else e.matmul(
                        p_b[:, j * 512:(j + 1) * 512],
                        wf4[sg][:, dc * 128:(dc + 1) * 128],
                        x2T[:, dc * 1024 + j * 512:dc * 1024 + (j + 1) * 512],
                        start=(dc == 0), stop=(dc == 31)))
            y_mm = sem_mm.take()
            wf_free[sg] = y_mm
            if pend is not None:
                p8_tail_b(pend)
            pend = (hb, oc, sg, y_mm, xs2_ld)
        y8_v, y4_v = p8_tail_a(pend)
        p8_tail_b(pend)
        OUT_DONE = sem_out.n



        wait("gpsimd", sem_out, OUT_DONE)
        wait("gpsimd", sem_ast)
        wait("gpsimd", sem_xrs)
        wait("gpsimd", sem_mstb[0], sem_mstb[0].n)
        wait("gpsimd", sem_mstb[1], sem_mstb[1].n)
        wait("gpsimd", sem_ln2b[0], sem_ln2b[0].n)
        wait("gpsimd", sem_ln2b[1], sem_ln2b[1].n)
        wait("gpsimd", sem_qst)
        wait("gpsimd", sem_vst)
        wait("gpsimd", sem_lnb[0], sem_lnb[0].n)
        wait("gpsimd", sem_lnb[1], sem_lnb[1].n)
        wait("gpsimd", sem_ev)
        for name, (ext, src) in dbg.items():
            em("gpsimd", lambda e, ext=ext, src=src: e.dma_start(
                out=ext, in_=src[:, :]).then_inc(sem_dbg.h, 16))
            sem_dbg.take(16)


        wait("sync", sem_out, OUT_DONE)
        wait("sync", sem_dbg)

        with nc.Block() as block:
            @block.sync
            def _(eng):
                for en, fn in ops:
                    if en == "sync":
                        fn(eng)

            @block.tensor
            def _(eng):
                for en, fn in ops:
                    if en == "tensor":
                        fn(eng)

            @block.vector
            def _(eng):
                for en, fn in ops:
                    if en == "vector":
                        fn(eng)

            @block.scalar
            def _(eng):
                for en, fn in ops:
                    if en == "scalar":
                        fn(eng)

            @block.gpsimd
            def _(eng):
                for en, fn in ops:
                    if en == "gpsimd":
                        fn(eng)

    return nc


def make_inputs(full):
    x = np.asarray(full["L0_attn_x"], dtype=np.float32)
    wq = np.asarray(full["L0_attn_wq"], dtype=np.float32)
    wk = np.asarray(full["L0_attn_wk"], dtype=np.float32)
    wv = np.asarray(full["L0_attn_wv"], dtype=np.float32)
    wo = np.ascontiguousarray(np.asarray(full["L0_attn_wo"], dtype=np.float32))
    qw = np.asarray(full["L0_ffn_qweight"])
    lut4 = np.asarray(full["L0_ffn_lut4"], dtype=np.float32)
    lut8 = np.asarray(full["L0_ffn_lut8"], dtype=np.float32)
    bias = np.ascontiguousarray(np.asarray(full["L0_ffn_bias"], dtype=np.float32))

    j = np.arange(0, HD, 2, dtype=np.float64) / HD
    invf = 1.0 / (10000.0 ** j)
    t = np.arange(T, dtype=np.float64)
    ang = np.outer(invf, t)
    ropeC = np.repeat(np.cos(ang), 2, axis=0).astype(np.float32)
    ropeS = np.repeat(np.sin(ang), 2, axis=0).astype(np.float32)
    ropeS[0::2, :] *= -1.0
    perm = np.zeros((HD, HD), dtype=np.float32)
    for p in range(HD):
        perm[p ^ 1, p] = 1.0
    ident = np.eye(HD, dtype=np.float32)
    dmask = np.zeros((HD, 8, 1024), dtype=np.float32)
    for o in range(8):
        for kk in range(HD):
            dmask[kk, o, :] = (o * 128 + kk <= np.arange(1024)).astype(np.float32)
    repl = np.zeros((8, HD), dtype=np.float32)
    for o8 in range(8):
        repl[o8, o8 * 16:(o8 + 1) * 16] = 1.0
    sumones = np.zeros((HD, 8), dtype=np.float32)
    for o8 in range(8):
        sumones[o8 * 16:(o8 + 1) * 16, o8] = 1.0
    mod16 = (np.arange(HD) % 16).astype(np.float32).reshape(HD, 1)
    onescol = np.ones((HD, 1), dtype=np.float32)
    onesrow = np.ones((1, HD), dtype=np.float32)

    maps = []
    for c in range(8):
        b, h = c // 2, c % 2
        maps.append({
            "x": np.ascontiguousarray(x[b]),
            "xh": np.ascontiguousarray(x[b, h * TH:(h + 1) * TH]),
            "wq_s": np.ascontiguousarray(wq[:, h * QD:(h + 1) * QD]),
            "wk_s": np.ascontiguousarray(wk[:, h * KVD:(h + 1) * KVD]),
            "wv_s": np.ascontiguousarray(wv[:, h * KVD:(h + 1) * KVD]),
            "wo": wo,
            "qw_s": np.ascontiguousarray(qw[:, c * OS:(c + 1) * OS, :]),
            "lut4g_s": np.ascontiguousarray(lut4[c * OS:(c + 1) * OS].reshape(NG, 128)),
            "lut8t_s": np.ascontiguousarray(
                lut8[c * OS:(c + 1) * OS].reshape(OS, 16, 16).transpose(0, 2, 1).reshape(OS, 256)),
            "bias": bias,
            "ropeC": ropeC,
            "ropeS": ropeS,
            "permswap": perm.astype(BF),
            "ident128": ident.astype(BF),
            "ident128f": ident,
            "diagmasks": dmask.reshape(HD, 8 * 1024).astype(BF),
            "replones": repl.astype(BF),
            "sumones": sumones.astype(BF),
            "mod16": mod16,
            "onescol": onescol.astype(BF),
            "onesrow": onesrow,
            "onesrow_bf": onesrow.astype(BF),
            "hoff": np.array([[h * TH]], dtype=np.int32),
        })
    return maps


_CACHE = {}


def kernel(**inputs):
    if "nc" not in _CACHE:
        _CACHE["nc"] = build()
    nc = _CACHE["nc"]
    maps = make_inputs(inputs)
    res = run_bass_kernel_spmd(nc, maps, core_ids=list(range(8)))
    out = np.zeros((4, T, D), dtype=np.float32)
    for c in range(8):
        b, h = c // 2, c % 2
        out[b, h * TH:(h + 1) * TH, :] = res.results[c]["out"]
    return out


# version 13 (final)
# speedup vs baseline: 1.1109x; 1.1109x over previous
import contextlib
import numpy as np
import ml_dtypes
import concourse.bass as bass
import concourse.mybir as mybir
from concourse.bass_utils import run_bass_kernel_spmd

F32 = mybir.dt.float32
BF16 = mybir.dt.bfloat16
I32 = mybir.dt.int32
I16 = mybir.dt.int16
I8 = mybir.dt.int8
AF = mybir.ActivationFunctionType
ALU = mybir.AluOpType
AX = mybir.AxisListType
BF = ml_dtypes.bfloat16

D = 4096
T = 4096
TH = 2048
TB = 1024
NB = T // TB
HD = 128
NQL = 16
NKVL = 4
QD = NQL * HD
KVD = NKVL * HD
OS = 512
NG = OS // 8

RS = 0.0883883476483184
ESHIFT = -12.0

ARENA_BYTES = 192 * 1024


class Sem:
    def __init__(self, cm, nc, name):
        self.h = cm.enter_context(nc.semaphore(name))
        self.n = 0

    def take(self, k=1):
        self.n += k
        return self.n


class Arena:

    def __init__(self, base_ap):
        self.base = base_ap
        self.off = 0

    def alloc(self, cols, dt):
        nbytes = cols * mybir.dt.size(dt)
        nbytes = (nbytes + 63) & ~63
        assert self.off + nbytes <= ARENA_BYTES, (self.off, nbytes)
        w0 = self.off // 4
        self.off += nbytes
        return self.base[:, w0:w0 + nbytes // 4].bitcast(dt)[:, 0:cols]

    def at(self, byte_off, cols, dt):
        nbytes = cols * mybir.dt.size(dt)
        assert byte_off % 4 == 0 and byte_off + nbytes <= ARENA_BYTES
        w0 = byte_off // 4
        return self.base[:, w0:w0 + (nbytes + 3) // 4].bitcast(dt)[:, 0:cols]

    def mark(self):
        return self.off

    def reset(self, off=0):
        self.off = off


def build(debug=(), detect_races=True):
    nc = bass.Bass("TRN2", target_bir_lowering=False,
                   detect_race_conditions=detect_races)

    def dram(name, shape, dt, kind=None):
        if kind:
            return nc.dram_tensor(name, shape, dt, kind=kind).ap()
        return nc.dram_tensor(name, shape, dt)

    x_in = dram("x", [T, D], F32, "ExternalInput")
    xh_in = dram("xh", [TH, D], F32, "ExternalInput")
    wq_in = dram("wq_s", [D, QD], F32, "ExternalInput")
    wk_in = dram("wk_s", [D, KVD], F32, "ExternalInput")
    wv_in = dram("wv_s", [D, KVD], F32, "ExternalInput")
    wo_in = dram("wo", [D, D], F32, "ExternalInput")
    qw_in = dram("qw_s", [8, OS, 128], I32, "ExternalInput")
    lut4_in = dram("lut4g_s", [NG, 128], F32, "ExternalInput")
    lut8_in = dram("lut8t_s", [OS, 256], F32, "ExternalInput")
    bias_in = dram("bias", [D], F32, "ExternalInput")
    ropeC_in = dram("ropeC", [HD, T], F32, "ExternalInput")
    ropeS_in = dram("ropeS", [HD, T], F32, "ExternalInput")
    perm_in = dram("permswap", [HD, HD], BF16, "ExternalInput")
    ident_in = dram("ident128", [HD, HD], BF16, "ExternalInput")
    ident32_in = dram("ident128f", [HD, HD], F32, "ExternalInput")
    dmask_in = dram("diagmasks", [HD, 8 * 1024], BF16, "ExternalInput")
    repl_in = dram("replones", [8, HD], BF16, "ExternalInput")
    sumones_in = dram("sumones", [HD, 8], BF16, "ExternalInput")
    mod16_in = dram("mod16", [HD, 1], F32, "ExternalInput")
    onesc_in = dram("onescol", [HD, 1], BF16, "ExternalInput")
    onesr_in = dram("onesrow", [1, HD], F32, "ExternalInput")
    onesrb_in = dram("onesrow_bf", [1, HD], BF16, "ExternalInput")
    hoff_in = dram("hoff", [1, 1], I32, "ExternalInput")

    out_ext = dram("out", [TH, D], F32, "ExternalOutput")

    xln_d = dram("xln_d", [T, D], BF16)
    q_d = dram("q_d", [QD, T], BF16)
    k_d = dram("k_d", [KVD, T], BF16)
    v_d = dram("v_d", [T, KVD], BF16)
    att_myg = [dram(f"att_myg{g}", [512, T], BF16) for g in range(4)]
    att_gg = [dram(f"att_gg{g}", [1024, T], BF16) for g in range(4)]
    wbf_d = dram("wbf_d", [24 * 128, D], BF16)
    xres_d = dram("xres_d", [TH, D], F32)
    xln2_d = dram("xln2_d", [TH, D], BF16)
    mask_d = dram("mask_d", [TH, 1], BF16)
    lorep_d = dram("lorep_d", [16 * OS, D], BF16)
    hirep_d = dram("hirep_d", [16 * OS, D], BF16)
    l8bd_d = dram("l8bd_d", [NG, 128, 128], BF16)
    wsh_d = dram("wsh_d", [OS, 2 * D], BF16)
    wshT_d = dram("wshT_d", [D, 1024], BF16)
    wfull_d = dram("wfull_d", [8 * D, 1024], BF16)

    dbg = {}
    dbg_l4bc = None
    if "l4bc" in debug:
        dbg_l4bc = dram("dbg_l4bc", [128, 64], F32, "ExternalOutput")
        debug = tuple(d for d in debug if d != "l4bc")
    dbg_deq = None
    if "deqdump" in debug:
        dbg_deq = dram("dbg_deq", [128, 1536], BF16, "ExternalOutput")
        debug = tuple(d for d in debug if d != "deqdump")
    for name in debug:
        src = {"xln": xln_d, "q": q_d, "k": k_d, "v": v_d, "att": att_myg[0],
               "attg": att_gg[0], "xres": xres_d, "xln2": xln2_d,
               "wfull": wfull_d, "wsh": wsh_d, "mask": mask_d}[name]
        a = src[:, :]
        dbg[name] = (dram("dbg_" + name, [a.shape[0], a.shape[1]], a.dtype,
                          "ExternalOutput"), src)

    ops = []

    def em(engine, fn):
        ops.append((engine, fn))

    def wait(engine, sem, val=None):
        v = sem.n if val is None else val
        if v > 0:
            em(engine, lambda e, s=sem.h, v=v: e.wait_ge(s, v))

    with contextlib.ExitStack() as stack:
        ent = stack.enter_context

        def sbuf(name, shape, dt):
            return ent(nc.sbuf_tensor(name, shape, dt))

        def psum(name, shape, dt=F32):
            return ent(nc.psum_tensor(name, shape, dt))

        S = lambda name: Sem(stack, nc, name)

        consts = sbuf("consts", [128, 16], F32)
        stats = sbuf("stats", [128, 16], F32)
        perm_t = sbuf("perm_t", [128, 128], BF16)
        ident_t = sbuf("ident_t", [128, 128], BF16)
        ident32_t = sbuf("ident32_t", [128, 128], F32)
        mod16_t = sbuf("mod16_t", [128, 1], F32)
        hoffs_t = sbuf("hoffs_t", [1, 1], I32)
        onesrb_t = sbuf("onesrb_t", [1, 128], BF16)
        repl_t = sbuf("repl_t", [8, 128], BF16)
        sumones_t = sbuf("sumones_t", [128, 8], BF16)
        lut4g_t = sbuf("lut4g_t", [64, 128], F32)
        lut4bc = sbuf("lut4bc", [128, 64], F32)
        arena_t = sbuf("arena", [128, ARENA_BYTES // 4], F32)
        AR = Arena(arena_t[:, :])

        p_a = psum("p_a", [128, TB])
        p_b = psum("p_b", [128, TB])
        p_c = psum("p_c", [128, TB])
        p_d = psum("p_d", [128, TB])

        sem_xb = [S("sx0"), S("sx1")]
        sem_lnb = [S("sln0"), S("sln1")]
        sem_xt = S("sxt")
        sem_wb = [S("sw0"), S("sw1")]
        sem_mm = S("smm")
        sem_ev = S("sev")
        sem_evs = S("sevs")
        sem_st = S("sst")
        sem_qst = S("sqst")
        sem_vst = S("svst")
        sem_dbg = S("sdbg")
        sem_qh = S("sqh")
        sem_kg = S("skg")
        sem_vg = S("svg")
        sem_ast = S("sast")
        sem_cc = S("scc")
        sem_af = S("saf")
        sem_wob = [S("swo0"), S("swo1")]
        sem_xsb = [S("sxs0"), S("sxs1")]
        sem_xrs = S("sxrs")
        sem_x2b = [S("sx20"), S("sx21")]
        sem_ln2b = [S("sl20"), S("sl21")]
        sem_mstb = [S("smst0"), S("smst1")]
        sem_hf = S("shf")
        sem_qw = S("sqw")
        sem_lh = S("slh")
        sem_bd = S("sbd")
        sem_wsh = S("swsh")
        sem_ccw = S("sccw")
        sem_g8 = S("sg8")
        sem_trw = S("strw")
        sem_x2t = S("sx2t")
        sem_wf = [S("swf0"), S("swf1")]
        sem_xs2 = [S("sxs20"), S("sxs21")]
        sem_out = S("sout")
        sem_mrow = S("smrow")
        sem_l4 = S("sl4")
        sem_qb = [S(f"sqb{i}") for i in range(4)]
        sem_l8b = [S("sl8b0"), S("sl8b1")]
        sem_pe2 = S("spe2")
        sem_c = S("sc")
        sem_cv = S("scv")
        sem_vch = S("svch")
        sem_ach = S("sach")
        sem_wbs = [S("swbs0"), S("swbs1")]

        cl = [(0, 1.0 / D), (1, 1e-5), (2, ESHIFT), (3, RS),
              (4, 1.0), (5, 0.0), (6, -1.0), (7, 64.0), (8, -0.5)]
        for i, (col, val) in enumerate(cl):
            last = (i == len(cl) - 1)
            em("vector", lambda e, c=col, v=val, last=last: (
                e.memset(consts[:, c:c + 1], v).then_inc(sem_cv.h, 1) if last
                else e.memset(consts[:, c:c + 1], v)))
        sem_cv.take()
        em("sync", lambda e: e.dma_start(out=perm_t[:], in_=perm_in).then_inc(sem_c.h, 16))
        em("sync", lambda e: e.dma_start(out=ident_t[:], in_=ident_in).then_inc(sem_c.h, 16))
        em("sync", lambda e: e.dma_start(out=ident32_t[:], in_=ident32_in).then_inc(sem_c.h, 16))
        em("sync", lambda e: e.dma_start(out=mod16_t[:], in_=mod16_in).then_inc(sem_c.h, 16))
        sem_c.take(64)
        C_READY = sem_c.n


        xlnT = AR.at(0, 32 * TB, BF16)
        wstage = [AR.at(65536 + i * 8192, 32 * 128, BF16) for i in range(2)]
        ropeCb = AR.at(81920, TB, F32)
        ropeSb = AR.at(86016, TB, F32)
        qraw = AR.at(90112, TB, BF16)
        t1 = AR.at(92160, TB, BF16)
        t2 = AR.at(94208, TB, BF16)
        qrope = AR.at(96256, TB, BF16)
        vstage = AR.at(98304, TB, BF16)
        vtok = AR.at(100352, TB, BF16)
        qwstage = AR.at(102400, 8 * 128, I32)
        dq_tmp = AR.at(106496, D, I32)
        dq_tmp3 = AR.at(122880, D, I32)
        dq_acc = AR.at(139264, D, I32)
        lo_bf = AR.at(106496, D, BF16)
        hi_bf = AR.at(114688, D, BF16)
        lorep_t = [AR.at(102400 + i * 2048, 1024, BF16) for i in range(4)]
        hirep_t = [AR.at(110592 + i * 2048, 1024, BF16) for i in range(4)]
        l8bd_t = [AR.at(118784 + i * 256, 128, BF16) for i in range(2)]
        elo_t = [AR.at(119296 + i * 1024, 512, BF16) for i in range(4)]
        ehi_t = [AR.at(123392 + i * 1024, 512, BF16) for i in range(4)]
        e4_t2 = [AR.at(127488 + i * 1024, 512, BF16) for i in range(4)]
        p2_t = [AR.at(131584 + i * 1024, 512, BF16) for i in range(2)]
        w8row = AR.at(133632, D, BF16)
        w4row = AR.at(141824, D, BF16)
        tr_in = AR.at(102400, D, BF16)
        tr_out = AR.at(110592, D, BF16)
        xtile = [AR.at(155648 + i * 16384, D, F32) for i in range(2)]
        xlnt = [AR.at(188416, D, BF16)]
        zero_bf = AR.at(155648, D, BF16)

        def rawap(handle, dims, offset=0):
            return bass.AP(handle, offset, [list(d) for d in dims])

        em("sync", lambda e: e.dma_start(
            out=repl_t[0:8, :], in_=repl_in).then_inc(sem_c.h, 16))
        em("sync", lambda e: e.dma_start(
            out=sumones_t[:, :], in_=sumones_in).then_inc(sem_c.h, 16))
        sem_c.take(32)
        DQC = sem_c.n
        C_READY = DQC

        em("vector", lambda e: e.memset(zero_bf, 0.0).then_inc(sem_vch.h, 1))
        z_done = sem_vch.take()
        wait("gpsimd", sem_vch, z_done)
        for z in range(2):
            em("gpsimd", lambda e, z=z: e.dma_start(
                out=rawap(l8bd_d, [[128, 128], [16384, 32], [1, 128]],
                          offset=z * 32 * 16384),
                in_=zero_bf.rearrange("p (g c) -> p g c", c=128)).then_inc(sem_bd.h, 16))
            sem_bd.take(16)
        wait("gpsimd", sem_bd, sem_bd.n)
        for o8 in range(8):
            em("gpsimd", lambda e, o8=o8: e.dma_start(
                out=rawap(l8bd_d, [[16384, NG], [128, 16], [1, 16]], offset=o8 * 2064),
                in_=rawap(lut8_in.tensor, [[8 * 256, NG], [16, 16], [1, 16]],
                          offset=o8 * 256)).then_inc(sem_bd.h, 16))
            sem_bd.take(16)
        BD_DONE = sem_bd.n
        em("sync", lambda e: e.dma_start(
            out=lut4g_t[:, :], in_=lut4_in).then_inc(sem_l4.h, 16))
        l4_ld = sem_l4.take(16)
        wait("tensor", sem_l4, l4_ld)
        wait("tensor", sem_c, DQC)
        em("tensor", lambda e: e.transpose(
            p_d[:, 0:64], lut4g_t[:, :], ident32_t[0:64, 0:64]).then_inc(sem_pe2.h, 1))
        l4_tr = sem_pe2.take()
        wait("scalar", sem_pe2, l4_tr)
        em("scalar", lambda e: e.copy(lut4bc[:, :], p_d[:, 0:64]).then_inc(sem_ach.h, 1))
        L4BC = sem_ach.take()
        if dbg_l4bc is not None:
            wait("gpsimd", sem_ach, L4BC)
            em("gpsimd", lambda e: e.dma_start(
                out=dbg_l4bc, in_=lut4bc[:, :]).then_inc(sem_dbg.h, 16))
            sem_dbg.take(16)

        xln_written = []
        norm_done = []
        ZD = z_done

        def ln_gen():
            for t in range(32):
                buf = xtile[t % 2]
                sq = xlnt[0]
                sc = (t % 2) * 8
                if t >= 2:
                    wait("sync", sem_ev, norm_done[t - 2])
                elif t == 0:
                    wait("sync", sem_bd, 32)
                em("sync", lambda e, t=t, b=buf: e.dma_start(
                    out=b, in_=x_in[t * 128:(t + 1) * 128, :]).then_inc(sem_xb[t % 2].h, 16))
                x_ld = sem_xb[t % 2].take(16)
                wait("scalar", sem_xb[t % 2], x_ld)
                wait("scalar", sem_cv, 1)
                if t >= 1:
                    wait("scalar", sem_ev, norm_done[t - 1])
                    wait("scalar", sem_lnb[(t - 1) % 2], xln_written[t - 1])
                em("scalar", lambda e, b=buf, s=sq, sc=sc: e.activation(
                    s, b, AF.Square,
                    accum_out=stats[:, sc + 1:sc + 2]).then_inc(sem_ach.h, 1))
                sq_done = sem_ach.take()
                wait("vector", sem_xb[t % 2], x_ld)
                em("vector", lambda e, b=buf, sc=sc: e.reduce_sum(
                    stats[:, sc:sc + 1], b, axis=AX.X).then_inc(sem_vch.h, 1))
                v_sum = sem_vch.take()
                wait("vector", sem_ach, sq_done)
                em("vector", lambda e, sc=sc: e.tensor_scalar(
                    stats[:, sc + 3:sc + 4], stats[:, sc + 1:sc + 2], consts[:, 0:1],
                    None, ALU.mult).then_inc(sem_vch.h, 1))
                sem_vch.take()
                wait("vector", sem_vch, v_sum)
                em("vector", lambda e, sc=sc: e.tensor_scalar(
                    stats[:, sc + 2:sc + 3], stats[:, sc:sc + 1], consts[:, 0:1],
                    None, ALU.mult).then_inc(sem_vch.h, 1))
                v_mean = sem_vch.take()
                wait("vector", sem_vch, v_mean)
                em("vector", lambda e, sc=sc: e.scalar_tensor_tensor(
                    stats[:, sc + 4:sc + 5], stats[:, sc + 2:sc + 3], stats[:, sc + 2:sc + 3],
                    stats[:, sc + 3:sc + 4], ALU.mult, ALU.subtract).then_inc(sem_vch.h, 1))
                v_nv = sem_vch.take()
                wait("scalar", sem_vch, v_nv)
                em("scalar", lambda e, sc=sc: e.activation(
                    stats[:, sc + 5:sc + 6], stats[:, sc + 4:sc + 5], AF.Ln,
                    bias=consts[:, 1:2], scale=consts[:, 6:7]).then_inc(sem_ach.h, 1))
                a_ln = sem_ach.take()
                wait("scalar", sem_ach, a_ln)
                em("scalar", lambda e, sc=sc: e.activation(
                    stats[:, sc + 5:sc + 6], stats[:, sc + 5:sc + 6], AF.Exp,
                    scale=consts[:, 8:9]).then_inc(sem_ach.h, 1))
                a_rstd = sem_ach.take()
                wait("vector", sem_ach, a_rstd)
                if t >= 1:
                    wait("vector", sem_lnb[(t - 1) % 2], xln_written[t - 1])
                em("vector", lambda e, b=buf, sc=sc, xo=sq: e.tensor_scalar(
                    xo, b, stats[:, sc + 2:sc + 3], stats[:, sc + 5:sc + 6],
                    ALU.subtract, ALU.mult).then_inc(sem_ev.h, 1))
                norm_done.append(sem_ev.take())
                wait("gpsimd", sem_ev, norm_done[-1])
                em("gpsimd", lambda e, t=t, xo=sq: e.dma_start(
                    out=xln_d[t * 128:(t + 1) * 128, :], in_=xo).then_inc(sem_lnb[t % 2].h, 16))
                xln_written.append(sem_lnb[t % 2].take(16))
                yield

        def ext_gen():
            qw_consumed = 0
            lohi = []
            for ot in range(4):
                if qw_consumed:
                    wait("sync", sem_vch, qw_consumed)
                em("sync", lambda e, ot=ot: e.dma_start(
                    out=qwstage.rearrange("p (pl w) -> p pl w", pl=8),
                    in_=rawap(qw_in.tensor, [[128, 128], [OS * 128, 8], [1, 128]],
                              offset=ot * 128 * 128)).then_inc(sem_qw.h, 16))
                qw_ld = sem_qw.take(16)
                wait("vector", sem_qw, qw_ld)
                prev_stt = 0
                for p8 in range(8):
                    for s in range(32):
                        if prev_stt:
                            wait("vector", sem_vch, prev_stt)
                        em("vector", lambda e, p8=p8, s=s: e.tensor_scalar(
                            dq_tmp[:, s::32], qwstage[:, p8 * 128:(p8 + 1) * 128], s, 1,
                            ALU.logical_shift_right, ALU.bitwise_and).then_inc(sem_vch.h, 1))
                        sem_vch.take()
                    ex_done = sem_vch.n
                    wait("vector", sem_vch, ex_done)
                    if p8 == 0:
                        em("vector", lambda e: e.tensor_copy(
                            dq_acc, dq_tmp).then_inc(sem_vch.h, 1))
                        prev_stt = sem_vch.take()
                    else:
                        em("vector", lambda e, p8=p8: e.tensor_scalar(
                            dq_tmp3, dq_tmp, p8, None,
                            ALU.logical_shift_left).then_inc(sem_vch.h, 1))
                        v_sh = sem_vch.take()
                        wait("vector", sem_vch, v_sh)
                        em("vector", lambda e: e.tensor_tensor(
                            dq_acc, dq_acc, dq_tmp3, ALU.bitwise_or).then_inc(sem_vch.h, 1))
                        prev_stt = sem_vch.take()
                    yield
                wait("vector", sem_vch, prev_stt)
                em("vector", lambda e: e.tensor_scalar(
                    dq_tmp3, dq_acc, 15, None, ALU.bitwise_and).then_inc(sem_vch.h, 1))
                v1 = sem_vch.take()
                wait("vector", sem_vch, v1)
                em("vector", lambda e: e.tensor_copy(lo_bf, dq_tmp3).then_inc(sem_vch.h, 1))
                v2 = sem_vch.take()
                wait("vector", sem_vch, v2)
                em("vector", lambda e: e.tensor_scalar(
                    dq_tmp3, dq_acc, 4, None, ALU.logical_shift_right).then_inc(sem_vch.h, 1))
                v3 = sem_vch.take()
                wait("vector", sem_vch, v3)
                em("vector", lambda e: e.tensor_copy(hi_bf, dq_tmp3).then_inc(sem_vch.h, 1))
                v4 = sem_vch.take()
                qw_consumed = v4
                wait("gpsimd", sem_vch, v4)
                for rep in range(16):
                    em("gpsimd", lambda e, ot=ot, rep=rep: e.dma_start(
                        out=rawap(lorep_d, [[16 * D, 128], [1, D]],
                                  offset=(128 * ot * 16 + rep) * D),
                        in_=lo_bf).then_inc(sem_lh.h, 16))
                    em("gpsimd", lambda e, ot=ot, rep=rep: e.dma_start(
                        out=rawap(hirep_d, [[16 * D, 128], [1, D]],
                                  offset=(128 * ot * 16 + rep) * D),
                        in_=hi_bf).then_inc(sem_lh.h, 16))
                    lohi.append(sem_lh.take(32))
                if ot < 3:
                    wait("vector", sem_lh, lohi[-1])
                yield

        EXT_LH = [0, 0]

        def deq_gen():
            wsh_written = []
            eq_done = {}
            e4_done = {}
            s1_mm = {}
            s3_mm = {}
            p2_done = {}
            w_ev = {}
            q_ld = {}
            l8_ld = {}

            def load_quarter(qq):
                if qq >= 256:
                    return
                g, qi = qq // 4, qq % 4
                bb = qq % 4
                if qq >= 4 and (qq - 4) * 2 + 1 in eq_done:
                    wait("sync", sem_vch, e4_done[(qq - 4) * 2 + 1])
                em("sync", lambda e, g=g, qi=qi, bb=bb: e.dma_start(
                    out=lorep_t[bb],
                    in_=lorep_d[g * 128:(g + 1) * 128, qi * 1024:(qi + 1) * 1024]
                ).then_inc(sem_qb[bb].h, 16))
                em("sync", lambda e, g=g, qi=qi, bb=bb: e.dma_start(
                    out=hirep_t[bb],
                    in_=hirep_d[g * 128:(g + 1) * 128, qi * 1024:(qi + 1) * 1024]
                ).then_inc(sem_qb[bb].h, 16))
                q_ld[qq] = sem_qb[bb].take(32)

            def load_l8bd(g):
                if g >= NG:
                    return
                if g >= 2 and (g - 2, 7) in s1_mm:
                    wait("sync", sem_mm, s1_mm[(g - 2, 7)])
                em("sync", lambda e, g=g: e.dma_start(
                    out=l8bd_t[g % 2], in_=l8bd_d[g, :, :]).then_inc(sem_l8b[g % 2].h, 16))
                l8_ld[g] = sem_l8b[g % 2].take(16)

            wait("sync", sem_bd, BD_DONE)
            wait("sync", sem_lh, EXT_LH[0])
            wait("sync", sem_vch, EXT_LH[1])
            for qq in range(4):
                load_quarter(qq)
            load_l8bd(0)
            load_l8bd(1)

            def emit_s3(kp):
                g, ch = kp // 8, kp % 8
                b2 = kp % 2
                k = (g, ch)
                wait("tensor", sem_vch, p2_done[k])
                if kp >= 2:
                    wait("tensor", sem_ach, wev_flat[kp - 2])
                em("tensor", lambda e, b2=b2: e.matmul(
                    p_b[0:8, b2 * 512:(b2 + 1) * 512], sumones_t[:], p2_t[b2],
                    start=True, stop=True))
                em("tensor", lambda e, b2=b2, kp=kp: e.matmul(
                    p_b[32:40, b2 * 512:(b2 + 1) * 512], sumones_t[:], e4_t2[kp % 4],
                    start=True, stop=True,
                    tile_position=(0, 32)).then_inc(sem_mm.h, 1))
                s3_mm[k] = sem_mm.take()
                s3m_flat.append(s3_mm[k])
                last_s3_box[0] = s3_mm[k]
                cs = slice(ch * 512, (ch + 1) * 512)
                wait("scalar", sem_mm, s3_mm[k])
                if ch == 0 and wsh_written:
                    wait("scalar", sem_wsh, wsh_written[-1])
                em("scalar", lambda e, cs=cs, b2=b2: e.copy(
                    w8row[0:8, cs], p_b[0:8, b2 * 512:(b2 + 1) * 512]).then_inc(sem_ach.h, 1))
                sem_ach.take()
                em("scalar", lambda e, cs=cs, b2=b2: e.copy(
                    w4row[0:8, cs], p_b[32:40, b2 * 512:(b2 + 1) * 512]).then_inc(sem_ach.h, 1))
                w_ev[k] = sem_ach.take()
                wev_flat.append(w_ev[k])
                last_w_ev_box[0] = w_ev[k]
                if ch == 7:
                    load_l8bd(g + 2)
                    wait("gpsimd", sem_ach, w_ev[k])
                    em("gpsimd", lambda e, g=g: e.dma_start(
                        out=wsh_d[8 * g:8 * g + 8, 0:D],
                        in_=w8row[0:8, :]).then_inc(sem_wsh.h, 16))
                    em("gpsimd", lambda e, g=g: e.dma_start(
                        out=wsh_d[8 * g:8 * g + 8, D:2 * D],
                        in_=w4row[0:8, :]).then_inc(sem_wsh.h, 16))
                    wsh_written.append(sem_wsh.take(32))

            last_s3_box = [0]
            last_w_ev_box = [0]

            for kk in range(512):
                g, ch = kk // 8, kk % 8
                if dbg_deq is not None and kk == 2:
                    wait("tensor", sem_dbg, 48)
                    wait("vector", sem_dbg, 48)
                    wait("sync", sem_dbg, 48)
                qq = kk // 2
                coff = (kk % 2) * 512
                b4 = kk % 4
                b2 = kk % 2
                k = (g, ch)
                first = (kk == 0)
                wait("vector", sem_qb[qq % 4], q_ld[qq])
                if kk >= 4:
                    wait("vector", sem_mm, s1_mm_flat[kk - 4])
                    wait("vector", sem_vch, p2d_flat[kk - 4])
                em("vector", lambda e, b4=b4, qb=qq % 4, coff=coff: e.tensor_scalar(
                    elo_t[b4], lorep_t[qb][:, coff:coff + 512], mod16_t[:, 0:1], None,
                    ALU.is_equal).then_inc(sem_vch.h, 1))
                sem_vch.take()
                em("vector", lambda e, b4=b4, qb=qq % 4, coff=coff: e.tensor_scalar(
                    ehi_t[b4], hirep_t[qb][:, coff:coff + 512], mod16_t[:, 0:1], None,
                    ALU.is_equal).then_inc(sem_vch.h, 1))
                eq_done[kk] = sem_vch.take()
                wait("vector", sem_ach, L4BC)
                wait("vector", sem_vch, eq_done[kk])
                em("vector", lambda e, b4=b4, g=g: e.tensor_scalar(
                    e4_t2[b4], elo_t[b4], lut4bc[:, g:g + 1], None,
                    ALU.mult).then_inc(sem_vch.h, 1))
                e4_done[kk] = sem_vch.take()
                if kk % 2 == 1:
                    load_quarter(qq + 4)
                wait("tensor", sem_vch, e4_done[kk])
                if ch == 0:
                    wait("tensor", sem_l8b[g % 2], l8_ld[g])
                if kk == 0:
                    wait("tensor", sem_c, DQC)
                    wait("tensor", sem_ach, L4BC)
                if kk >= 2:
                    wait("tensor", sem_vch, p2d_flat[kk - 2])
                em("tensor", lambda e, b2=b2, g=g, kk=kk: e.matmul(
                    p_d[:, b2 * 512:(b2 + 1) * 512], l8bd_t[g % 2], elo_t[kk % 4],
                    start=True, stop=True).then_inc(sem_mm.h, 1))
                s1_mm[k] = sem_mm.take()
                s1_mm_flat.append(s1_mm[k])
                wait("vector", sem_mm, s1_mm[k])
                if kk >= 2:
                    wait("vector", sem_mm, s3m_flat[kk - 2])
                em("vector", lambda e, b2=b2, kk=kk: e.tensor_tensor(
                    p2_t[b2], p_d[:, b2 * 512:(b2 + 1) * 512], ehi_t[kk % 4],
                    ALU.mult).then_inc(sem_vch.h, 1))
                p2_done[k] = sem_vch.take()
                p2d_flat.append(p2_done[k])
                if dbg_deq is not None and kk == 0:
                    wait("gpsimd", sem_vch, p2_done[k])
                    em("gpsimd", lambda e: e.dma_start(
                        out=dbg_deq[:, 0:512], in_=elo_t[0]).then_inc(sem_dbg.h, 16))
                    em("gpsimd", lambda e: e.dma_start(
                        out=dbg_deq[:, 512:1024], in_=ehi_t[0]).then_inc(sem_dbg.h, 16))
                    em("gpsimd", lambda e: e.dma_start(
                        out=dbg_deq[:, 1024:1536], in_=p2_t[0]).then_inc(sem_dbg.h, 16))
                    sem_dbg.take(48)

                if kk >= 1:
                    emit_s3(kk - 1)
                yield
            emit_s3(511)
            tr_ld_prev = 0
            tr_st_prev = 0
            tr_ev = 0
            for wi in range(2):
                for ot in range(4):
                    wait("sync", sem_wsh, wsh_written[-1])
                    wait("sync", sem_mm, last_s3_box[0])
                    wait("sync", sem_vch, p2d_flat[-1])
                    if tr_st_prev:
                        wait("sync", sem_trw, tr_st_prev)
                    em("sync", lambda e, wi=wi, ot=ot: e.dma_start(
                        out=tr_in, in_=wsh_d[ot * 128:(ot + 1) * 128, wi * D:(wi + 1) * D]
                    ).then_inc(sem_qw.h, 16))
                    tr_ld = sem_qw.take(16)
                    wait("tensor", sem_qw, tr_ld)
                    if wi == 0 and ot == 0:
                        wait("tensor", sem_ach, last_w_ev_box[0])
                    if tr_ld_prev:
                        wait("tensor", sem_ach, tr_ev)
                    for ic in range(32):
                        em("tensor", lambda e, ic=ic: e.transpose(
                            p_d[:].bitcast(BF16)[:, (ic % 16) * 128:(ic % 16 + 1) * 128],
                            tr_in[:, ic * 128:(ic + 1) * 128], ident_t[:]
                        ).then_inc(sem_pe2.h, 1) if ic % 16 == 15 else e.transpose(
                            p_d[:].bitcast(BF16)[:, (ic % 16) * 128:(ic % 16 + 1) * 128],
                            tr_in[:, ic * 128:(ic + 1) * 128], ident_t[:]))
                        if ic % 16 == 15:
                            half_tr = sem_pe2.take()
                            wait("scalar", sem_pe2, half_tr)
                            h0 = (ic // 16) * 16 * 128
                            if tr_st_prev:
                                wait("scalar", sem_trw, tr_st_prev)
                            em("scalar", lambda e, h0=h0: e.copy(
                                tr_out[:, h0:h0 + 2048],
                                p_d[:].bitcast(BF16)[:, 0:2048]).then_inc(sem_ach.h, 1))
                            tr_ev = sem_ach.take()
                            wait("tensor", sem_ach, tr_ev)
                    tr_ld_prev = tr_ld
                    wait("gpsimd", sem_ach, tr_ev)
                    em("gpsimd", lambda e, wi=wi, ot=ot: e.dma_start(
                        out=rawap(wshT_d, [[1024, 128], [128 * 1024, 32], [1, 128]],
                                  offset=wi * 512 + ot * 128),
                        in_=tr_out.rearrange("p (ic c) -> p ic c", c=128)).then_inc(sem_trw.h, 16))
                    tr_st_prev = sem_trw.take(16)
                    yield
            wait("gpsimd", sem_trw, sem_trw.n)
            em("gpsimd", lambda e: e.collective_compute(
                "AllGather", ALU.bypass, replica_groups=[list(range(8))],
                ins=[wshT_d.ap().opt()], outs=[wfull_d.ap().opt()]).then_inc(sem_ccw.h, 1))
            sem_ccw.take()
            yield

        s1_mm_flat = []
        p2d_flat = []
        s3m_flat = []
        wev_flat = []

        CCW_BOX = [0]

        def phaseA_gen():
            psum_free = [0, 0]
            psw_free = 0
            qraw_free = 0
            qraw_free_pe2 = 0
            vstage_free = 0
            stage_free = [0, 0]
            wbs_val = [0, 0]
            blk_mm_done = 0
            blk_pe2_done = 0
            pend_rope = None

            def emit_rope_tail(p):
                nonlocal qraw_free, qraw_free_pe2, psw_free, vstage_free
                nonlocal qr_store, vt_store
                knd, pg, idx, tok0, kind, ev_q, prev_qraw_free = p
                if knd == "qk":
                    wait("tensor", sem_evs, ev_q)
                    wait("tensor", sem_c, C_READY)
                    for j in range(2):
                        em("tensor", lambda e, j=j, psw=p_qk[pg], last=(j == 1): (
                            e.matmul(
                                psw[:, j * 512:(j + 1) * 512], perm_t[:],
                                qraw[:, j * 512:(j + 1) * 512], start=True, stop=True
                            ).then_inc(sem_pe2.h, 1) if last else e.matmul(
                                psw[:, j * 512:(j + 1) * 512], perm_t[:],
                                qraw[:, j * 512:(j + 1) * 512], start=True, stop=True)))
                    qraw_free_pe2 = sem_pe2.take()
                    wait("vector", sem_evs, ev_q)
                    wait("vector", sem_ev, prev_qraw_free)
                    em("vector", lambda e: e.tensor_tensor(
                        t1, qraw, ropeCb, ALU.mult).then_inc(sem_vch.h, 1))
                    v_t1 = sem_vch.take()
                    wait("vector", sem_pe2, qraw_free_pe2)
                    em("vector", lambda e, psw=p_qk[pg]: e.tensor_tensor(
                        t2, psw[:], ropeSb, ALU.mult).then_inc(sem_vch.h, 1))
                    v_t2 = sem_vch.take()
                    psum_vch[pg] = v_t2
                    if qr_store:
                        wait("vector", sem_qst, qr_store)
                    wait("vector", sem_vch, v_t2)
                    em("vector", lambda e: e.tensor_tensor(
                        qrope, t1, t2, ALU.add).then_inc(sem_ev.h, 1))
                    psw_free = sem_ev.take()
                    qraw_free = psw_free
                    wait("gpsimd", sem_ev, qraw_free)
                    dst = q_d if kind == "q" else k_d
                    em("gpsimd", lambda e, dst=dst, idx=idx, tok0=tok0: e.dma_start(
                        out=dst[idx * 128:(idx + 1) * 128, tok0:tok0 + TB],
                        in_=qrope).then_inc(sem_qst.h, 16))
                    qr_store = sem_qst.take(16)
                else:
                    wait("tensor", sem_evs, ev_q)
                    for tt in range(TB // 128):
                        em("tensor", lambda e, tt=tt, psw=p_qk[pg],
                           last=(tt == TB // 128 - 1): (
                            e.transpose(
                                psw[:].bitcast(BF16)[:, tt * 128:(tt + 1) * 128],
                                vstage[:, tt * 128:(tt + 1) * 128], ident_t[:]
                            ).then_inc(sem_pe2.h, 1) if last else e.transpose(
                                psw[:].bitcast(BF16)[:, tt * 128:(tt + 1) * 128],
                                vstage[:, tt * 128:(tt + 1) * 128], ident_t[:])))
                    qraw_free_pe2 = sem_pe2.take()
                    wait("vector", sem_pe2, qraw_free_pe2)
                    if vt_store:
                        wait("vector", sem_vst, vt_store)
                    em("vector", lambda e, psw=p_qk[pg]: e.tensor_copy(
                        vtok, psw[:].bitcast(BF16)[:, 0:TB]).then_inc(sem_ev.h, 1))
                    psw_free = sem_ev.take()
                    vstage_free = psw_free
                    psum_ev2[pg] = psw_free
                    wait("gpsimd", sem_ev, vstage_free)
                    em("gpsimd", lambda e, idx=idx, tok0=tok0: e.dma_start(
                        out=v_d[tok0:tok0 + TB, idx * 128:(idx + 1) * 128].rearrange(
                            "(tt p) c -> p tt c", p=128),
                        in_=vtok.rearrange("p (tt c) -> p tt c", c=128)
                    ).then_inc(sem_vst.h, 16))
                    vt_store = sem_vst.take(16)
            pre_w = {}
            for ci0 in range(2):
                for s4 in range(4):
                    em("gpsimd", lambda e, j=ci0, sg=ci0, s4=s4,
                       sh=sem_wb[ci0].h: e.dma_start(
                        out=wstage[sg][:, s4 * 1024:(s4 + 1) * 1024].rearrange(
                            "p (c f) -> p c f", c=8),
                        in_=wq_in[s4 * 1024:(s4 + 1) * 1024,
                                  j * 128:(j + 1) * 128].rearrange(
                            "(c p) f -> p c f", p=128)).then_inc(sh, 16))
                pre_w[ci0] = sem_wb[ci0].take(64)
            yield
            p_qk = [p_a, p_c]
            psum_vch = [0, 0]
            psum_ev2 = [0, 0]
            qr_store = 0
            vt_store = 0
            for blk in range(NB):
                tok0 = blk * TB
                for tt in range(8 * blk, 8 * blk + 8):
                    wait("sync", sem_lnb[tt % 2], xln_written[tt])
                if blk > 0:
                    wait("sync", sem_mm, blk_mm_done)
                    wait("sync", sem_pe2, blk_pe2_done)
                    wait("sync", sem_ev, qraw_free)
                    wait("sync", sem_ev, vstage_free)
                for dc in range(32):
                    em("sync", lambda e, dc=dc, tok0=tok0: e.dma_start(
                        out=xlnT[:, dc * TB:(dc + 1) * TB],
                        in_=xln_d[tok0:tok0 + TB, dc * 128:(dc + 1) * 128],
                        transpose=True).then_inc(sem_xt.h, 16))
                    sem_xt.take(16)
                em("sync", lambda e, tok0=tok0: e.dma_start(
                    out=ropeCb, in_=ropeC_in[:, tok0:tok0 + TB]).then_inc(sem_xt.h, 16))
                em("sync", lambda e, tok0=tok0: e.dma_start(
                    out=ropeSb, in_=ropeS_in[:, tok0:tok0 + TB]).then_inc(sem_xt.h, 16))
                sem_xt.take(32)
                xt_done = sem_xt.n
                yield

                chunks = [("q", i) for i in range(16)] + [("k", i) for i in range(4)] \
                    + [("v", i) for i in range(4)]
                for ci, (kind, idx) in enumerate(chunks):
                    pg = ci % 2
                    sg = ci % 2
                    win = {"q": wq_in, "k": wk_in, "v": wv_in}[kind]
                    if blk == 0 and ci < 2:
                        w_ld = pre_w[ci]
                        wait("gpsimd", sem_wb[sg], w_ld)
                        em("gpsimd", lambda e, ci=ci, sg=sg: e.dma_start(
                            out=wbf_d[ci * 128:(ci + 1) * 128, :],
                            in_=wstage[sg]).then_inc(sem_wbs[sg].h, 16))
                        wbs_val[sg] = sem_wbs[sg].take(16)
                    elif blk == 0:
                        wait("gpsimd", sem_mm, stage_free[sg])
                        wait("gpsimd", sem_wbs[sg], wbs_val[sg])
                        for s4 in range(4):
                            em("gpsimd", lambda e, w=win, j=idx, sg=sg, s4=s4: e.dma_start(
                                out=wstage[sg][:, s4 * 1024:(s4 + 1) * 1024].rearrange(
                                    "p (c f) -> p c f", c=8),
                                in_=w[s4 * 1024:(s4 + 1) * 1024,
                                      j * 128:(j + 1) * 128].rearrange(
                                    "(c p) f -> p c f", p=128)
                            ).then_inc(sem_wb[sg].h, 16))
                        w_ld = sem_wb[sg].take(64)
                        wait("gpsimd", sem_wb[sg], w_ld)
                        em("gpsimd", lambda e, ci=ci, sg=sg: e.dma_start(
                            out=wbf_d[ci * 128:(ci + 1) * 128, :],
                            in_=wstage[sg]).then_inc(sem_wbs[sg].h, 16))
                        wbs_val[sg] = sem_wbs[sg].take(16)
                    else:
                        wait("sync", sem_mm, stage_free[sg])
                        wait("sync", sem_wbs[sg], wbs_val[sg])
                        for s2 in range(2):
                            em("sync", lambda e, ci=ci, sg=sg, s2=s2: e.dma_start(
                                out=wstage[sg][:, s2 * 2048:(s2 + 1) * 2048],
                                in_=wbf_d[ci * 128:(ci + 1) * 128,
                                          s2 * 2048:(s2 + 1) * 2048]
                            ).then_inc(sem_wb[sg].h, 16))
                        w_ld = sem_wb[sg].take(32)
                    wait("tensor", sem_wb[sg], w_ld)
                    wait("tensor", sem_xt, xt_done)
                    wait("tensor", sem_evs, psum_free[pg])
                    wait("tensor", sem_vch, psum_vch[pg])
                    wait("tensor", sem_ev, psum_ev2[pg])
                    for d in range(32):
                        for j in range(2):
                            last = (d == 31 and j == 1)
                            em("tensor", lambda e, d=d, j=j, pg=pg, sg=sg, last=last: (
                                e.matmul(
                                    p_qk[pg][:, j * 512:(j + 1) * 512],
                                    wstage[sg][:, d * 128:(d + 1) * 128],
                                    xlnT[:, d * TB + j * 512:d * TB + (j + 1) * 512],
                                    start=(d == 0), stop=(d == 31)
                                ).then_inc(sem_mm.h, 1) if last else e.matmul(
                                    p_qk[pg][:, j * 512:(j + 1) * 512],
                                    wstage[sg][:, d * 128:(d + 1) * 128],
                                    xlnT[:, d * TB + j * 512:d * TB + (j + 1) * 512],
                                    start=(d == 0), stop=(d == 31))))
                    stage_free[sg] = sem_mm.take()
                    mm_done = sem_mm.n

                    if pend_rope is not None:
                        emit_rope_tail(pend_rope)
                        pend_rope = None
                    if kind in ("q", "k"):
                        prev_qraw_free = qraw_free
                        wait("scalar", sem_mm, mm_done)
                        wait("scalar", sem_pe2, qraw_free_pe2)
                        wait("scalar", sem_ev, qraw_free)
                        em("scalar", lambda e, pg=pg: e.copy(
                            qraw, p_qk[pg][:]).then_inc(sem_evs.h, 1))
                        psum_free[pg] = sem_evs.take()
                        pend_rope = ("qk", pg, idx, tok0, kind,
                                     psum_free[pg], prev_qraw_free)
                    else:
                        wait("scalar", sem_mm, mm_done)
                        wait("scalar", sem_pe2, qraw_free_pe2)
                        wait("scalar", sem_ev, vstage_free)
                        em("scalar", lambda e, pg=pg: e.copy(
                            vstage, p_qk[pg][:]).then_inc(sem_evs.h, 1))
                        psum_free[pg] = sem_evs.take()
                        pend_rope = ("v", pg, idx, tok0, kind,
                                     psum_free[pg], 0)
                    yield
                if pend_rope is not None:
                    emit_rope_tail(pend_rope)
                    pend_rope = None
                blk_mm_done = sem_mm.n
                blk_pe2_done = sem_pe2.n

        lg = ln_gen()
        eg = ext_gen()
        ag = phaseA_gen()
        next(ag)
        for _ in range(8):
            next(lg)
        next(ag)
        dg = None
        alive_a, alive_l, alive_e, alive_d = True, True, True, False
        while alive_a or alive_l or alive_e or alive_d:
            if alive_a:
                try:
                    next(ag)
                except StopIteration:
                    alive_a = False
            if alive_l:
                try:
                    next(lg)
                except StopIteration:
                    alive_l = False
            if alive_e:
                try:
                    next(eg)
                except StopIteration:
                    alive_e = False
                    EXT_LH[0] = sem_lh.n
                    EXT_LH[1] = sem_vch.n
                    dg = deq_gen()
                    alive_d = True
            if alive_d:
                ndq = 8 if alive_a else 1000000
                for _ in range(ndq):
                    try:
                        next(dg)
                    except StopIteration:
                        alive_d = False
                        break
        CCW = sem_ccw.n

        AR.reset(0)
        qh_t = [AR.alloc(T, BF16) for _ in range(2)]
        kg_t = [AR.alloc(T, BF16) for _ in range(2)]
        vg_t = [AR.alloc(T, BF16) for _ in range(2)]
        pt_t = [AR.alloc(1024, BF16) for _ in range(4)]
        ptsum_t = AR.alloc(1024, F32)
        rb_t = AR.alloc(1024, F32)
        asb_t = AR.alloc(1024, BF16)
        den_sb = AR.alloc(1024, F32)
        dm_t = AR.alloc(8 * 1024, BF16)
        onesr_t = AR.alloc(128, F32)

        p_st = [p_a, p_b]
        p_at2 = [p_c, p_d]

        A_EV = sem_ev.n
        A_EVS = sem_evs.n
        A_MM = sem_mm.n
        A_PE2 = sem_pe2.n
        A_QST = sem_qst.n
        A_VST = sem_vst.n
        A_ACH = sem_ach.n
        A_VCH = sem_vch.n
        for eng in ("sync", "tensor", "vector", "scalar", "gpsimd"):
            wait(eng, sem_ev, A_EV)
            wait(eng, sem_evs, A_EVS)
            wait(eng, sem_mm, A_MM)
            wait(eng, sem_pe2, A_PE2)
            wait(eng, sem_qst, A_QST)
            wait(eng, sem_vst, A_VST)
            wait(eng, sem_ach, A_ACH)
            wait(eng, sem_vch, A_VCH)

        em("sync", lambda e: e.dma_start(out=dm_t, in_=dmask_in).then_inc(sem_c.h, 16))
        em("sync", lambda e: e.dma_start(
            out=onesr_t[0:1, :], in_=onesr_in).then_inc(sem_c.h, 16))
        sem_c.take(32)
        C2_READY = sem_c.n

        qh_ld = {}
        kg_ld = {}
        vg_ld = {}

        def load_qh(lh, wait_mm):
            wait("sync", sem_mm, wait_mm)
            em("sync", lambda e, lh=lh: e.dma_start(
                out=qh_t[lh % 2], in_=q_d[lh * 128:(lh + 1) * 128, :]
            ).then_inc(sem_qh.h, 16))
            qh_ld[lh] = sem_qh.take(16)

        def load_kv(gi, wait_mm):
            wait("sync", sem_mm, wait_mm)
            em("sync", lambda e, gi=gi: e.dma_start(
                out=kg_t[gi % 2], in_=k_d[gi * 128:(gi + 1) * 128, :]
            ).then_inc(sem_kg.h, 16))
            kg_ld[gi] = sem_kg.take(16)
            em("sync", lambda e, gi=gi: e.dma_start(
                out=vg_t[gi % 2].rearrange("p (tt c) -> p tt c", c=128),
                in_=v_d[:, gi * 128:(gi + 1) * 128].rearrange(
                    "(tt p) c -> p tt c", p=128)).then_inc(sem_vg.h, 16))
            vg_ld[gi] = sem_vg.take(16)

        load_kv(0, 0)
        load_qh(0, 0)
        load_qh(1, 0)

        segs = [(gi, a, 4 * gi + a, qc)
                for gi in range(NKVL) for a in range(4) for qc in range(4)]
        bank_free = [0, 0]
        slot_av = [0, 0, 0, 0]
        slot_ps = [0, 0, 0, 0]
        av_last = {}
        att_ev_vals = {-1: 0}
        ast_vals = {-1: 0}
        rb_vals = {-1: 0}
        den_vals = {-1: 0}
        tail_prev = None

        def tail_den(P):
            k, gi, a, qc, ptsum_final, av_end = P
            pn = (k + 1) % 2
            wait("tensor", sem_vch, ptsum_final)
            wait("tensor", sem_ev, att_ev_vals[k - 1])
            for j in range(2):
                em("tensor", lambda e, j=j, pn=pn, last=(j == 1): (
                    e.matmul(p_at2[pn][0:1, j * 512:(j + 1) * 512], consts[:, 4:5],
                             ptsum_t[:, j * 512:(j + 1) * 512],
                             start=True, stop=True).then_inc(sem_mm.h, 1) if last
                    else e.matmul(p_at2[pn][0:1, j * 512:(j + 1) * 512],
                                  consts[:, 4:5],
                                  ptsum_t[:, j * 512:(j + 1) * 512],
                                  start=True, stop=True)))
            den_vals[k] = sem_mm.take()

        def tail_recip(P):
            k = P[0]
            pn = (k + 1) % 2
            wait("scalar", sem_mm, den_vals[k])
            em("scalar", lambda e, pn=pn: e.activation(
                den_sb[0:1, :], p_at2[pn][0:1, :], AF.Ln).then_inc(sem_ach.h, 1))
            sem_ach.take()
            em("scalar", lambda e: e.activation(
                den_sb[0:1, :], den_sb[0:1, :], AF.Exp,
                scale=consts[0:1, 6:7]).then_inc(sem_ach.h, 1))
            return sem_ach.take()

        def tail_rest(P, a_rcp):
            k, gi, a, qc, ptsum_final, av_end = P
            pp = k % 2
            pn = (k + 1) % 2
            wait("tensor", sem_ach, a_rcp)
            wait("tensor", sem_c, C2_READY)
            for j in range(2):
                em("tensor", lambda e, j=j, pn=pn, last=(j == 1): (
                    e.matmul(p_at2[pn][:, j * 512:(j + 1) * 512], onesr_t[0:1, :],
                             den_sb[0:1, j * 512:(j + 1) * 512],
                             start=True, stop=True).then_inc(sem_mm.h, 1) if last
                    else e.matmul(p_at2[pn][:, j * 512:(j + 1) * 512],
                                  onesr_t[0:1, :],
                                  den_sb[0:1, j * 512:(j + 1) * 512],
                                  start=True, stop=True)))
            bc_mm = sem_mm.take()
            wait("scalar", sem_mm, bc_mm)
            wait("scalar", sem_ev, att_ev_vals[k - 1])
            em("scalar", lambda e, pn=pn: e.copy(
                rb_t, p_at2[pn][:]).then_inc(sem_ach.h, 1))
            rb_vals[k] = sem_ach.take()
            wait("vector", sem_ach, rb_vals[k])
            wait("vector", sem_mm, av_end)
            wait("vector", sem_ast, ast_vals[k - 1])
            em("vector", lambda e, pp=pp: e.tensor_tensor(
                asb_t, p_at2[pp][:], rb_t, ALU.mult).then_inc(sem_ev.h, 1))
            att_ev_vals[k] = sem_ev.take()
            wait("sync", sem_ev, att_ev_vals[k])
            em("sync", lambda e, gi=gi, a=a, qc=qc: e.dma_start(
                out=att_myg[gi][a * 128:(a + 1) * 128,
                                qc * 1024:(qc + 1) * 1024],
                in_=asb_t).then_inc(sem_ast.h, 16))
            ast_vals[k] = sem_ast.take(16)
            if a == 3 and qc == 3:
                wait("gpsimd", sem_ast, ast_vals[k])
                em("gpsimd", lambda e, gi=gi: e.collective_compute(
                    "AllGather", ALU.bypass,
                    replica_groups=[[0, 1], [2, 3], [4, 5], [6, 7]],
                    ins=[att_myg[gi].ap().opt()],
                    outs=[att_gg[gi].ap().opt()]).then_inc(sem_cc.h, 1))
                sem_cc.take()

        for k, (gi, a, lh, qc) in enumerate(segs):
            nkt = 8 * qc + 8
            if k == 0:
                wait("vector", sem_c, C2_READY)
                wait("scalar", sem_cv, 1)
                wait("tensor", sem_cv, 1)
            if qc == 0:
                if lh + 1 < NQL:
                    load_qh(lh + 1, av_last.get(lh - 1, 0))
                if a == 2 and gi + 1 < NKVL:
                    load_kv(gi + 1, av_last.get(4 * (gi - 1) + 3, 0))
            S_val = {}
            exp_val = {}
            pready = {}

            def emit_S(kt, _lh=lh, _gi=gi, _qc=qc, _k=k):
                wait("tensor", sem_ach, bank_free[kt % 2])
                if _qc == 0 and kt == 0:
                    wait("tensor", sem_qh, qh_ld[_lh])
                    wait("tensor", sem_kg, kg_ld[_gi])
                    wait("tensor", sem_vg, vg_ld[_gi])
                for j in range(2):
                    em("tensor", lambda e, kt=kt, j=j, lh=_lh, gi=_gi, qc=_qc,
                       last=(j == 1): (
                        e.matmul(p_st[kt % 2][:, j * 512:(j + 1) * 512],
                                 kg_t[gi % 2][:, kt * 128:(kt + 1) * 128],
                                 qh_t[lh % 2][:, qc * 1024 + j * 512:
                                              qc * 1024 + (j + 1) * 512],
                                 start=True, stop=True).then_inc(sem_mm.h, 1)
                        if last else
                        e.matmul(p_st[kt % 2][:, j * 512:(j + 1) * 512],
                                 kg_t[gi % 2][:, kt * 128:(kt + 1) * 128],
                                 qh_t[lh % 2][:, qc * 1024 + j * 512:
                                              qc * 1024 + (j + 1) * 512],
                                 start=True, stop=True)))
                S_val[kt] = sem_mm.take()

            def emit_exp(kt):
                wait("scalar", sem_mm, S_val[kt])
                wait("scalar", sem_mm, slot_av[kt % 4])
                wait("scalar", sem_vch, slot_ps[kt % 4])
                em("scalar", lambda e, kt=kt: e.activation(
                    pt_t[kt % 4], p_st[kt % 2][:], AF.Exp, bias=consts[:, 2:3],
                    scale=consts[:, 3:4]).then_inc(sem_ach.h, 1))
                exp_val[kt] = sem_ach.take()
                bank_free[kt % 2] = exp_val[kt]
                pready[kt] = (sem_ach, exp_val[kt])

            def emit_mask(kt, _qc=qc):
                o = kt - 8 * _qc
                wait("vector", sem_ach, exp_val[kt])
                em("vector", lambda e, kt=kt, o=o: e.tensor_tensor(
                    pt_t[kt % 4], pt_t[kt % 4], dm_t[:, o * 1024:(o + 1) * 1024],
                    ALU.mult).then_inc(sem_vch.h, 1))
                pready[kt] = (sem_vch, sem_vch.take())

            P = tail_prev
            if P is not None:
                tail_den(P)
            emit_S(0)
            emit_S(1)
            if P is not None:
                a_rcp = tail_recip(P)
            emit_exp(0)
            emit_exp(1)
            emit_S(2)
            emit_S(3)
            if P is not None:
                tail_rest(P, a_rcp)
            if 0 >= nkt - 8:
                emit_mask(0)
            if 1 >= nkt - 8:
                emit_mask(1)

            av_end = 0
            for kt in range(nkt):
                psem, pval = pready[kt]
                wait("tensor", psem, pval)
                if kt == 0:
                    wait("tensor", sem_ach, rb_vals[k - 1])
                for j in range(2):
                    em("tensor", lambda e, kt=kt, j=j, gi=gi, k=k, nkt=nkt,
                       last=(j == 1): (
                        e.matmul(p_at2[k % 2][:, j * 512:(j + 1) * 512],
                                 vg_t[gi % 2][:, kt * 128:(kt + 1) * 128],
                                 pt_t[kt % 4][:, j * 512:(j + 1) * 512],
                                 start=(kt == 0), stop=(kt == nkt - 1)
                                 ).then_inc(sem_mm.h, 1)
                        if last else
                        e.matmul(p_at2[k % 2][:, j * 512:(j + 1) * 512],
                                 vg_t[gi % 2][:, kt * 128:(kt + 1) * 128],
                                 pt_t[kt % 4][:, j * 512:(j + 1) * 512],
                                 start=(kt == 0), stop=(kt == nkt - 1))))
                avv = sem_mm.take()
                slot_av[kt % 4] = avv
                if kt == nkt - 1:
                    av_end = avv
                if kt + 2 < nkt:
                    emit_exp(kt + 2)
                if kt + 4 < nkt:
                    emit_S(kt + 4)
                if kt == 0:
                    wait("vector", sem_mm, den_vals[k - 1])
                    if kt < nkt - 8:
                        wait("vector", sem_ach, exp_val[kt])
                    em("vector", lambda e: e.tensor_copy(
                        ptsum_t, pt_t[0]).then_inc(sem_vch.h, 1))
                else:
                    if kt < nkt - 8:
                        wait("vector", sem_ach, exp_val[kt])
                    em("vector", lambda e, kt=kt: e.tensor_tensor(
                        ptsum_t, ptsum_t, pt_t[kt % 4],
                        ALU.add).then_inc(sem_vch.h, 1))
                slot_ps[kt % 4] = sem_vch.take()
                if kt + 2 < nkt and kt + 2 >= nkt - 8:
                    emit_mask(kt + 2)
            if qc == 3:
                av_last[lh] = av_end
            tail_prev = (k, gi, a, qc, slot_ps[(nkt - 1) % 4], av_end)

        P = tail_prev
        tail_den(P)
        a_rcp = tail_recip(P)
        tail_rest(P, a_rcp)
        CC_ATT = sem_cc.n

        B_EV = sem_ev.n
        B_MM = sem_mm.n
        B_ACH = sem_ach.n
        B_VCH = sem_vch.n
        B_AST = sem_ast.n
        for eng in ("sync", "tensor", "vector", "scalar", "gpsimd"):
            wait(eng, sem_ev, B_EV)
            wait(eng, sem_mm, B_MM)
            wait(eng, sem_ach, B_ACH)
            wait(eng, sem_vch, B_VCH)
            wait(eng, sem_ast, B_AST)
            wait(eng, sem_cc, CC_ATT)

        AR.reset(0)
        attf = AR.alloc(32 * TH, BF16)
        wost = [AR.alloc(32 * 128, BF16) for _ in range(2)]
        ysb = AR.alloc(TH, F32)
        xstrip = [AR.alloc(TH, F32) for _ in range(2)]
        xres_sb = AR.alloc(TH, F32)

        em("gpsimd", lambda e: e.dma_start(
            out=hoffs_t[0:1, 0:1], in_=hoff_in).then_inc(sem_hf.h, 16))
        sem_hf.take(16)
        wait("gpsimd", sem_hf, sem_hf.n)
        hv_cell = [None]
        em("gpsimd", lambda e: hv_cell.__setitem__(
            0, nc.values_load(hoffs_t[0:1, 0:1], engines=[mybir.EngineType.Pool],
                              skip_runtime_bounds_check=True)))
        for ad in range(32):
            gi4 = (ad % 16) // 4
            roff = (ad // 16) * 512 + (ad % 4) * 128
            em("gpsimd", lambda e, ad=ad, gi4=gi4, roff=roff: e.dma_start(
                out=attf[:, ad * TH:(ad + 1) * TH],
                in_=att_gg[gi4][roff:roff + 128, bass.ds(hv_cell[0], TH)]
            ).then_inc(sem_af.h, 16))
            sem_af.take(16)
        AF_LD = sem_af.n

        ysb_free = 0
        tr_done = 0
        add_done = 0
        xr_store = 0
        xs_ld = [0, 0]
        wo_stage_free = [0, 0]
        xr_stores = []
        xs_consumed = {}
        for cc in range(32):
            sg = cc % 2
            wait("gpsimd", sem_mm, wo_stage_free[sg])
            em("gpsimd", lambda e, cc=cc, sg=sg: e.dma_start(
                out=wost[sg].rearrange("p (c f) -> p c f", c=32),
                in_=wo_in[:, cc * 128:(cc + 1) * 128].rearrange(
                    "(c p) f -> p c f", p=128)).then_inc(sem_wob[sg].h, 16))
            wo_ld = sem_wob[sg].take(16)
            wait("sync", sem_ev, add_done if cc < 2 else xs_consumed[cc - 2])
            em("sync", lambda e, cc=cc, sg=sg: e.dma_start(
                out=xstrip[sg].rearrange("p (tt c) -> p tt c", c=128),
                in_=xh_in[:, cc * 128:(cc + 1) * 128].rearrange(
                    "(tt p) c -> p tt c", p=128)).then_inc(sem_xsb[sg].h, 16))
            xs_ld[sg] = sem_xsb[sg].take(16)
            wait("tensor", sem_wob[sg], wo_ld)
            wait("tensor", sem_af, AF_LD)
            wait("tensor", sem_ach, ysb_free)
            for ad in range(32):
                for j in range(4):
                    pt = [p_a, p_b][j // 2]
                    em("tensor", lambda e, ad=ad, j=j, pt=pt, sg=sg: e.matmul(
                        pt[:, (j % 2) * 512:(j % 2 + 1) * 512],
                        wost[sg][:, ad * 128:(ad + 1) * 128],
                        attf[:, ad * TH + j * 512:ad * TH + (j + 1) * 512],
                        start=(ad == 0), stop=(ad == 31)
                    ).then_inc(sem_mm.h, 1) if (ad == 31 and j == 3) else e.matmul(
                        pt[:, (j % 2) * 512:(j % 2 + 1) * 512],
                        wost[sg][:, ad * 128:(ad + 1) * 128],
                        attf[:, ad * TH + j * 512:ad * TH + (j + 1) * 512],
                        start=(ad == 0), stop=(ad == 31)))
            y_mm = sem_mm.take()
            wo_stage_free[sg] = y_mm
            wait("scalar", sem_mm, y_mm)
            wait("scalar", sem_pe2, tr_done)
            em("scalar", lambda e: e.copy(ysb[:, 0:1024], p_a[:]).then_inc(sem_ach.h, 1))
            sem_ach.take()
            em("scalar", lambda e: e.copy(ysb[:, 1024:2048], p_b[:]).then_inc(sem_ach.h, 1))
            ysb_free = sem_ach.take()
            wait("tensor", sem_ach, ysb_free)
            wait("tensor", sem_ev, add_done)
            for tt in range(16):
                pt = [p_c, p_d][tt // 8]
                em("tensor", lambda e, tt=tt, pt=pt: e.transpose(
                    pt[:, (tt % 8) * 128:(tt % 8 + 1) * 128],
                    ysb[:, tt * 128:(tt + 1) * 128], ident32_t[:]
                ).then_inc(sem_pe2.h, 1) if tt == 15 else e.transpose(
                    pt[:, (tt % 8) * 128:(tt % 8 + 1) * 128],
                    ysb[:, tt * 128:(tt + 1) * 128], ident32_t[:]))
            tr_done = sem_pe2.take()
            wait("vector", sem_pe2, tr_done)
            wait("vector", sem_xsb[sg], xs_ld[sg])
            if xr_stores:
                wait("vector", sem_xrs, xr_stores[-1])
            em("vector", lambda e, sg=sg: e.tensor_tensor(
                xres_sb[:, 0:1024], p_c[:], xstrip[sg][:, 0:1024], ALU.add).then_inc(sem_vch.h, 1))
            sem_vch.take()
            em("vector", lambda e, sg=sg: e.tensor_tensor(
                xres_sb[:, 1024:2048], p_d[:], xstrip[sg][:, 1024:2048],
                ALU.add).then_inc(sem_ev.h, 1))
            add_done = sem_ev.take()
            xs_consumed[cc] = add_done
            wait("sync", sem_ev, add_done)
            em("sync", lambda e, cc=cc: e.dma_start(
                out=xres_d[:, cc * 128:(cc + 1) * 128].rearrange(
                    "(tt p) c -> p tt c", p=128),
                in_=xres_sb.rearrange("p (tt c) -> p tt c", c=128)).then_inc(sem_xrs.h, 16))
            xr_stores.append(sem_xrs.take(16))
        WO_DONE = sem_xrs.n

        C_EV = sem_ev.n
        C_MM = sem_mm.n
        C_ACH = sem_ach.n
        for eng in ("sync", "tensor", "vector", "scalar", "gpsimd"):
            wait(eng, sem_ev, C_EV)
            wait(eng, sem_mm, C_MM)
            wait(eng, sem_ach, C_ACH)
            wait(eng, sem_xrs, WO_DONE)
        AR.reset(65536)
        x2tile = [AR.alloc(D, F32) for _ in range(2)]
        x2ln = [AR.alloc(D, BF16) for _ in range(2)]
        mcol = AR.alloc(16, F32)
        mcol_bf = AR.alloc(16, BF16)
        x2T_pre = AR.at(0, 32 * 1024, BF16)

        ln2_written = []
        norm2_done = []
        mst_vals = []
        for t in range(16):
            buf = x2tile[t % 2]
            xo = x2ln[t % 2]
            sc = (t % 2) * 8
            if t >= 2:
                wait("sync", sem_ev, norm2_done[t - 2])
            for s4 in range(4):
                em("sync", lambda e, t=t, b=buf, s4=s4: e.dma_start(
                    out=b[:, s4 * 1024:(s4 + 1) * 1024],
                    in_=xres_d[t * 128:(t + 1) * 128, s4 * 1024:(s4 + 1) * 1024]
                ).then_inc(sem_x2b[t % 2].h, 16))
            x2_ld = sem_x2b[t % 2].take(64)
            wait("scalar", sem_x2b[t % 2], x2_ld)
            if t >= 2:
                wait("scalar", sem_ev, norm2_done[t - 2])
                wait("scalar", sem_ln2b[t % 2], ln2_written[t - 2])
            em("scalar", lambda e, b=buf, s=xo, sc=sc: e.activation(
                s, b, AF.Square,
                accum_out=stats[:, sc + 1:sc + 2]).then_inc(sem_ach.h, 1))
            sq_done = sem_ach.take()
            wait("vector", sem_x2b[t % 2], x2_ld)
            em("vector", lambda e, b=buf, sc=sc: e.reduce_sum(
                stats[:, sc:sc + 1], b, axis=AX.X).then_inc(sem_vch.h, 1))
            v_sum = sem_vch.take()
            wait("vector", sem_ach, sq_done)
            em("vector", lambda e, sc=sc: e.tensor_scalar(
                stats[:, sc + 3:sc + 4], stats[:, sc + 1:sc + 2], consts[:, 0:1],
                None, ALU.mult).then_inc(sem_vch.h, 1))
            sem_vch.take()
            wait("vector", sem_vch, v_sum)
            em("vector", lambda e, sc=sc: e.tensor_scalar(
                stats[:, sc + 2:sc + 3], stats[:, sc:sc + 1], consts[:, 0:1],
                None, ALU.mult).then_inc(sem_vch.h, 1))
            v_mean = sem_vch.take()
            wait("vector", sem_vch, v_mean)
            em("vector", lambda e, sc=sc: e.scalar_tensor_tensor(
                stats[:, sc + 4:sc + 5], stats[:, sc + 2:sc + 3], stats[:, sc + 2:sc + 3],
                stats[:, sc + 3:sc + 4], ALU.mult, ALU.subtract).then_inc(sem_vch.h, 1))
            v_nv = sem_vch.take()
            wait("vector", sem_vch, v_nv)
            em("vector", lambda e, sc=sc: e.tensor_scalar(
                stats[:, sc + 6:sc + 7], stats[:, sc + 4:sc + 5], consts[:, 6:7],
                consts[:, 1:2], ALU.mult, ALU.add).then_inc(sem_vch.h, 1))
            v_vpe = sem_vch.take()
            wait("vector", sem_vch, v_vpe)
            em("vector", lambda e, sc=sc: e.tensor_scalar(
                stats[:, sc + 7:sc + 8], stats[:, sc + 4:sc + 5], consts[:, 6:7],
                None, ALU.mult).then_inc(sem_vch.h, 1))
            v_var = sem_vch.take()
            wait("vector", sem_vch, v_var)
            if t >= 2:
                wait("vector", sem_mstb[t % 2], mst_vals[t - 2])
            em("vector", lambda e, sc=sc, t=t: e.tensor_tensor(
                mcol_bf[:, t % 2:t % 2 + 1], stats[:, sc + 7:sc + 8],
                stats[:, sc + 6:sc + 7], ALU.is_gt).then_inc(sem_vch.h, 1))
            v_msk = sem_vch.take()
            wait("sync", sem_vch, v_msk)
            em("sync", lambda e, t=t: e.dma_start(
                out=mask_d[t * 128:(t + 1) * 128, :],
                in_=mcol_bf[:, t % 2:t % 2 + 1]).then_inc(sem_mstb[t % 2].h, 16))
            mst_vals.append(sem_mstb[t % 2].take(16))
            wait("scalar", sem_vch, v_nv)
            em("scalar", lambda e, sc=sc: e.activation(
                stats[:, sc + 5:sc + 6], stats[:, sc + 4:sc + 5], AF.Ln,
                bias=consts[:, 1:2], scale=consts[:, 6:7]).then_inc(sem_ach.h, 1))
            a_ln = sem_ach.take()
            wait("scalar", sem_ach, a_ln)
            em("scalar", lambda e, sc=sc: e.activation(
                stats[:, sc + 5:sc + 6], stats[:, sc + 5:sc + 6], AF.Exp,
                scale=consts[:, 8:9]).then_inc(sem_ach.h, 1))
            a_rstd = sem_ach.take()
            wait("vector", sem_ach, a_rstd)
            if t >= 2:
                wait("vector", sem_ln2b[t % 2], ln2_written[t - 2])
            em("vector", lambda e, b=buf, sc=sc, xo=xo: e.tensor_scalar(
                xo, b, stats[:, sc + 2:sc + 3], stats[:, sc + 5:sc + 6],
                ALU.subtract, ALU.mult).then_inc(sem_ev.h, 1))
            norm2_done.append(sem_ev.take())
            wait("gpsimd", sem_ev, norm2_done[-1])
            em("gpsimd", lambda e, t=t, xo=xo: e.dma_start(
                out=xln2_d[t * 128:(t + 1) * 128, :], in_=xo).then_inc(sem_ln2b[t % 2].h, 16))
            ln2_written.append(sem_ln2b[t % 2].take(16))
        LN2_TOTALS = (sem_ln2b[0].n, sem_ln2b[1].n)

        wait("sync", sem_ln2b[0], ln2_written[6])
        wait("sync", sem_ln2b[1], ln2_written[7])
        for dc in range(32):
            em("sync", lambda e, dc=dc: e.dma_start(
                out=x2T_pre[:, dc * 1024:(dc + 1) * 1024],
                in_=xln2_d[0:1024, dc * 128:(dc + 1) * 128],
                transpose=True).then_inc(sem_x2t.h, 16))
            sem_x2t.take(16)

        E_EV = sem_ev.n
        E_VCH = sem_vch.n
        E_ACH = sem_ach.n
        E_MM = sem_mm.n
        E_PE2 = sem_pe2.n
        for eng in ("sync", "tensor", "vector", "scalar", "gpsimd"):
            wait(eng, sem_ev, E_EV)
            wait(eng, sem_vch, E_VCH)
            wait(eng, sem_ach, E_ACH)
            wait(eng, sem_mm, E_MM)
            wait(eng, sem_pe2, E_PE2)
            wait(eng, sem_ccw, CCW)
            wait(eng, sem_ln2b[0], LN2_TOTALS[0])
            wait(eng, sem_ln2b[1], LN2_TOTALS[1])
            wait(eng, sem_mstb[0], sem_mstb[0].n)
            wait(eng, sem_mstb[1], sem_mstb[1].n)
        AR.reset(0)
        x2T = AR.alloc(32 * 1024, BF16)
        wf8 = [AR.alloc(32 * 128, BF16) for _ in range(2)]
        wf4 = [AR.alloc(32 * 128, BF16) for _ in range(2)]
        maskb = AR.alloc(TH, I8)
        mrow = AR.alloc(TH, BF16)
        y8sb = AR.alloc(1024, F32)
        y4sb = AR.alloc(1024, F32)
        ysel = AR.alloc(1024, F32)
        xs2 = [AR.alloc(1024, F32) for _ in range(2)]
        outsb = AR.alloc(1024, F32)
        bias_sb = AR.alloc(32, F32)

        em("sync", lambda e: e.dma_start(
            out=mrow[0:1, :], in_=rawap(mask_d, [[TH, 1], [1, TH]])).then_inc(sem_mrow.h, 16))
        em("sync", lambda e: e.dma_start(
            out=bias_sb, in_=rawap(bias_in.tensor, [[1, 128], [128, 32]]),
            allow_slow_non_contiguous=True).then_inc(sem_mrow.h, 16))
        em("sync", lambda e: e.dma_start(
            out=onesrb_t[0:1, :], in_=onesrb_in).then_inc(sem_mrow.h, 16))
        sem_mrow.take(48)
        MR = sem_mrow.n
        wait("tensor", sem_mrow, MR)
        for j in range(4):
            pt = [p_c, p_d][j // 2]
            em("tensor", lambda e, j=j, pt=pt: e.matmul(
                pt[:, (j % 2) * 512:(j % 2 + 1) * 512], onesrb_t[0:1, :],
                mrow[0:1, j * 512:(j + 1) * 512], start=True, stop=True
            ).then_inc(sem_mm.h, 1) if j == 3 else e.matmul(
                pt[:, (j % 2) * 512:(j % 2 + 1) * 512], onesrb_t[0:1, :],
                mrow[0:1, j * 512:(j + 1) * 512], start=True, stop=True))
        mb_mm = sem_mm.take()
        wait("scalar", sem_mm, mb_mm)
        em("scalar", lambda e: e.copy(maskb[:, 0:1024], p_c[:]).then_inc(sem_ach.h, 1))
        sem_ach.take()
        em("scalar", lambda e: e.copy(maskb[:, 1024:2048], p_d[:]).then_inc(sem_ach.h, 1))
        mb_ev = sem_ach.take()

        out_sel_done = 0
        tr2_done = 0
        add2_done = 0
        y_ev = 0
        wf_free = [0, 0]
        xs2_last = [0, 0]
        out_stores = []
        X2T_HB = [0, 0]
        X2T_HB[0] = sem_x2t.n
        pend = None
        y8_v = 0
        y4_v = 0

        def p8_tail_a(p):
            global_unused = None
            hbp, ocp, sgp, y_mmp, _ = p
            wait("scalar", sem_mm, y_mmp)
            wait("scalar", sem_mrow, MR)
            wait("scalar", sem_vch, out_sel_vals[0])
            em("scalar", lambda e, oc=ocp: e.add(
                y8sb, p_a[:], bias_sb[:, oc:oc + 1]).then_inc(sem_ach.h, 1))
            y8v = sem_ach.take()
            wait("vector", sem_mm, y_mmp)
            em("vector", lambda e, oc=ocp: e.tensor_scalar(
                y4sb, p_b[:], bias_sb[:, oc:oc + 1], None,
                ALU.add).then_inc(sem_evs.h, 1))
            y4v = sem_evs.take()
            wait("vector", sem_ach, mb_ev)
            wait("vector", sem_pe2, tr2_vals[0])
            em("vector", lambda e: e.tensor_copy(ysel, y4sb).then_inc(sem_vch.h, 1))
            sem_vch.take()
            wait("vector", sem_ach, y8v)
            em("vector", lambda e, hb=hbp: e.copy_predicated(
                ysel, maskb[:, hb * 1024:(hb + 1) * 1024],
                y8sb).then_inc(sem_vch.h, 1))
            out_sel_vals[0] = sem_vch.take()
            return y8v, y4v

        def p8_tail_b(p):
            hbp, ocp, sgp, y_mmp, xs2_ldp = p
            wait("tensor", sem_vch, out_sel_vals[0])
            wait("tensor", sem_ev, add2_vals[0])
            wait("tensor", sem_ach, mb_ev)
            for tt in range(8):
                em("tensor", lambda e, tt=tt: e.transpose(
                    p_c[:, tt * 128:(tt + 1) * 128],
                    ysel[:, tt * 128:(tt + 1) * 128], ident32_t[:]
                ).then_inc(sem_pe2.h, 1) if tt == 7 else e.transpose(
                    p_c[:, tt * 128:(tt + 1) * 128],
                    ysel[:, tt * 128:(tt + 1) * 128], ident32_t[:]))
            tr2_vals[0] = sem_pe2.take()
            wait("vector", sem_pe2, tr2_vals[0])
            wait("vector", sem_xs2[sgp], xs2_ldp)
            if out_stores:
                wait("vector", sem_out, out_stores[-1])
            em("vector", lambda e, sg=sgp: e.tensor_tensor(
                outsb, p_c[:], xs2[sg], ALU.add).then_inc(sem_ev.h, 1))
            add2_vals[0] = sem_ev.take()
            xs2_last[sgp] = add2_vals[0]
            wait("sync", sem_ev, add2_vals[0])
            em("sync", lambda e, hb=hbp, oc=ocp: e.dma_start(
                out=rawap(out_ext.tensor, [[D, 128], [128 * D, 8], [1, 128]],
                          offset=hb * 1024 * D + oc * 128),
                in_=outsb.rearrange("p (tt c) -> p tt c", c=128)).then_inc(sem_out.h, 16))
            out_stores.append(sem_out.take(16))

        out_sel_vals = [0]
        tr2_vals = [0]
        add2_vals = [0]

        for tg in range(64):
            hb, oc = tg // 32, tg % 32
            sg = oc % 2
            if hb == 1 and oc == 0:
                wait("sync", sem_mm, sem_mm.n)
                for dc in range(32):
                    em("sync", lambda e, dc=dc: e.dma_start(
                        out=x2T[:, dc * 1024:(dc + 1) * 1024],
                        in_=xln2_d[1024:2048, dc * 128:(dc + 1) * 128],
                        transpose=True).then_inc(sem_x2t.h, 16))
                    sem_x2t.take(16)
                X2T_HB[1] = sem_x2t.n
            wait("sync", sem_mm, wf_free[sg])
            r, wc = oc // 4, (oc % 4) * 128
            for h2 in range(2):
                em("sync", lambda e, r=r, wc=wc, sg=sg, h2=h2: e.dma_start(
                    out=wf8[sg][:, h2 * 2048:(h2 + 1) * 2048].rearrange(
                        "p (c f) -> p c f", c=16),
                    in_=rawap(wfull_d, [[1024, 128], [128 * 1024, 16], [1, 128]],
                              offset=r * D * 1024 + wc + h2 * 16 * 128 * 1024)
                ).then_inc(sem_wf[sg].h, 16))
                em("sync", lambda e, r=r, wc=wc, sg=sg, h2=h2: e.dma_start(
                    out=wf4[sg][:, h2 * 2048:(h2 + 1) * 2048].rearrange(
                        "p (c f) -> p c f", c=16),
                    in_=rawap(wfull_d, [[1024, 128], [128 * 1024, 16], [1, 128]],
                              offset=r * D * 1024 + 512 + wc + h2 * 16 * 128 * 1024)
                ).then_inc(sem_wf[sg].h, 16))
            wf_ld = sem_wf[sg].take(64)
            wait("sync", sem_ev, xs2_last[sg])
            em("sync", lambda e, hb=hb, oc=oc, sg=sg: e.dma_start(
                out=xs2[sg].rearrange("p (tt c) -> p tt c", c=128),
                in_=rawap(xres_d, [[D, 128], [128 * D, 8], [1, 128]],
                          offset=hb * 1024 * D + oc * 128)).then_inc(sem_xs2[sg].h, 16))
            xs2_ld = sem_xs2[sg].take(16)
            if pend is not None:
                y8_v, y4_v = p8_tail_a(pend)
            wait("tensor", sem_wf[sg], wf_ld)
            wait("tensor", sem_x2t, X2T_HB[hb])
            wait("tensor", sem_ach, y8_v)
            wait("tensor", sem_evs, y4_v)
            for dc in range(32):
                for j in range(2):
                    em("tensor", lambda e, dc=dc, j=j, sg=sg: e.matmul(
                        p_a[:, j * 512:(j + 1) * 512],
                        wf8[sg][:, dc * 128:(dc + 1) * 128],
                        x2T[:, dc * 1024 + j * 512:dc * 1024 + (j + 1) * 512],
                        start=(dc == 0), stop=(dc == 31)))
                for j in range(2):
                    em("tensor", lambda e, dc=dc, j=j, sg=sg: e.matmul(
                        p_b[:, j * 512:(j + 1) * 512],
                        wf4[sg][:, dc * 128:(dc + 1) * 128],
                        x2T[:, dc * 1024 + j * 512:dc * 1024 + (j + 1) * 512],
                        start=(dc == 0), stop=(dc == 31)
                    ).then_inc(sem_mm.h, 1) if (dc == 31 and j == 1) else e.matmul(
                        p_b[:, j * 512:(j + 1) * 512],
                        wf4[sg][:, dc * 128:(dc + 1) * 128],
                        x2T[:, dc * 1024 + j * 512:dc * 1024 + (j + 1) * 512],
                        start=(dc == 0), stop=(dc == 31)))
            y_mm = sem_mm.take()
            wf_free[sg] = y_mm
            if pend is not None:
                p8_tail_b(pend)
            pend = (hb, oc, sg, y_mm, xs2_ld)
        y8_v, y4_v = p8_tail_a(pend)
        p8_tail_b(pend)
        OUT_DONE = sem_out.n



        wait("gpsimd", sem_out, OUT_DONE)
        wait("gpsimd", sem_ast)
        wait("gpsimd", sem_xrs)
        wait("gpsimd", sem_mstb[0], sem_mstb[0].n)
        wait("gpsimd", sem_mstb[1], sem_mstb[1].n)
        wait("gpsimd", sem_ln2b[0], sem_ln2b[0].n)
        wait("gpsimd", sem_ln2b[1], sem_ln2b[1].n)
        wait("gpsimd", sem_qst)
        wait("gpsimd", sem_vst)
        wait("gpsimd", sem_lnb[0], sem_lnb[0].n)
        wait("gpsimd", sem_lnb[1], sem_lnb[1].n)
        wait("gpsimd", sem_ev)
        for name, (ext, src) in dbg.items():
            em("gpsimd", lambda e, ext=ext, src=src: e.dma_start(
                out=ext, in_=src[:, :]).then_inc(sem_dbg.h, 16))
            sem_dbg.take(16)


        wait("sync", sem_out, OUT_DONE)
        wait("sync", sem_dbg)

        with nc.Block() as block:
            @block.sync
            def _(eng):
                for en, fn in ops:
                    if en == "sync":
                        fn(eng)

            @block.tensor
            def _(eng):
                for en, fn in ops:
                    if en == "tensor":
                        fn(eng)

            @block.vector
            def _(eng):
                for en, fn in ops:
                    if en == "vector":
                        fn(eng)

            @block.scalar
            def _(eng):
                for en, fn in ops:
                    if en == "scalar":
                        fn(eng)

            @block.gpsimd
            def _(eng):
                for en, fn in ops:
                    if en == "gpsimd":
                        fn(eng)

    return nc


def make_inputs(full):
    x = np.asarray(full["L0_attn_x"], dtype=np.float32)
    wq = np.asarray(full["L0_attn_wq"], dtype=np.float32)
    wk = np.asarray(full["L0_attn_wk"], dtype=np.float32)
    wv = np.asarray(full["L0_attn_wv"], dtype=np.float32)
    wo = np.ascontiguousarray(np.asarray(full["L0_attn_wo"], dtype=np.float32))
    qw = np.asarray(full["L0_ffn_qweight"])
    lut4 = np.asarray(full["L0_ffn_lut4"], dtype=np.float32)
    lut8 = np.asarray(full["L0_ffn_lut8"], dtype=np.float32)
    bias = np.ascontiguousarray(np.asarray(full["L0_ffn_bias"], dtype=np.float32))

    j = np.arange(0, HD, 2, dtype=np.float64) / HD
    invf = 1.0 / (10000.0 ** j)
    t = np.arange(T, dtype=np.float64)
    ang = np.outer(invf, t)
    ropeC = np.repeat(np.cos(ang), 2, axis=0).astype(np.float32)
    ropeS = np.repeat(np.sin(ang), 2, axis=0).astype(np.float32)
    ropeS[0::2, :] *= -1.0
    perm = np.zeros((HD, HD), dtype=np.float32)
    for p in range(HD):
        perm[p ^ 1, p] = 1.0
    ident = np.eye(HD, dtype=np.float32)
    dmask = np.zeros((HD, 8, 1024), dtype=np.float32)
    for o in range(8):
        for kk in range(HD):
            dmask[kk, o, :] = (o * 128 + kk <= np.arange(1024)).astype(np.float32)
    repl = np.zeros((8, HD), dtype=np.float32)
    for o8 in range(8):
        repl[o8, o8 * 16:(o8 + 1) * 16] = 1.0
    sumones = np.zeros((HD, 8), dtype=np.float32)
    for o8 in range(8):
        sumones[o8 * 16:(o8 + 1) * 16, o8] = 1.0
    mod16 = (np.arange(HD) % 16).astype(np.float32).reshape(HD, 1)
    onescol = np.ones((HD, 1), dtype=np.float32)
    onesrow = np.ones((1, HD), dtype=np.float32)

    maps = []
    for c in range(8):
        b, h = c // 2, c % 2
        maps.append({
            "x": np.ascontiguousarray(x[b]),
            "xh": np.ascontiguousarray(x[b, h * TH:(h + 1) * TH]),
            "wq_s": np.ascontiguousarray(wq[:, h * QD:(h + 1) * QD]),
            "wk_s": np.ascontiguousarray(wk[:, h * KVD:(h + 1) * KVD]),
            "wv_s": np.ascontiguousarray(wv[:, h * KVD:(h + 1) * KVD]),
            "wo": wo,
            "qw_s": np.ascontiguousarray(qw[:, c * OS:(c + 1) * OS, :]),
            "lut4g_s": np.ascontiguousarray(lut4[c * OS:(c + 1) * OS].reshape(NG, 128)),
            "lut8t_s": np.ascontiguousarray(
                lut8[c * OS:(c + 1) * OS].reshape(OS, 16, 16).transpose(0, 2, 1).reshape(OS, 256)),
            "bias": bias,
            "ropeC": ropeC,
            "ropeS": ropeS,
            "permswap": perm.astype(BF),
            "ident128": ident.astype(BF),
            "ident128f": ident,
            "diagmasks": dmask.reshape(HD, 8 * 1024).astype(BF),
            "replones": repl.astype(BF),
            "sumones": sumones.astype(BF),
            "mod16": mod16,
            "onescol": onescol.astype(BF),
            "onesrow": onesrow,
            "onesrow_bf": onesrow.astype(BF),
            "hoff": np.array([[h * TH]], dtype=np.int32),
        })
    return maps


_CACHE = {}


def kernel(**inputs):
    if "nc" not in _CACHE:
        _CACHE["nc"] = build()
    nc = _CACHE["nc"]
    maps = make_inputs(inputs)
    res = run_bass_kernel_spmd(nc, maps, core_ids=list(range(8)))
    out = np.zeros((4, T, D), dtype=np.float32)
    for c in range(8):
        b, h = c // 2, c % 2
        out[b, h * TH:(h + 1) * TH, :] = res.results[c]["out"]
    return out
